# version 09
# baseline (speedup 1.0000x reference)
import numpy as np
import ml_dtypes
import concourse.bass as bass
import concourse.mybir as mybir
from concourse.bass_utils import run_bass_kernel_spmd
from contextlib import ExitStack

F32 = mybir.dt.float32
BF16 = mybir.dt.bfloat16
U8 = mybir.dt.uint8
AF = mybir.ActivationFunctionType
ALU = mybir.AluOpType
AX = mybir.AxisListType

S = 4096
D = 1024
HD = 1408
NE = 8
NC8 = 8
NJ = 11
NH = 16
TT = 512
NT = S // TT
HALF = 2048
LN_EPS = 1e-5
ATT_EPS = 1e-6
BIG = 1e30
NCORES = 8

DEBUG_OUTPUTS = False
DEBUG_DUMPS = False
PHASES = ("A", "B1", "B2", "C")
SAFE_MODE = False

SEM_NAMES = ["dG", "dX", "dX2", "dStg", "dS2", "dS3", "dW1", "dW3", "dW2",
             "dCache", "dC2", "dOut", "dO2", "dB", "cPe", "cAct", "cDve", "cPool"]
ENGINES = ("sync", "tensor", "vector", "scalar", "gpsimd")
BAR_SEMS = ["cPe", "cAct", "cDve", "cPool", "dX", "dX2", "dStg", "dS2", "dS3",
            "dW1", "dW3", "dW2", "dCache", "dC2", "dOut", "dO2", "dB", "dG"]


class Sched:
    def __init__(self):
        self.segments = [{e: [] for e in ENGINES}]
        self.cnt = {}
        self.guard = {e: [] for e in ENGINES}

    def new_segment(self):
        self.segments.append({e: [] for e in ENGINES})

    def ev(self, sem):
        return (sem, self.cnt.get(sem, 0))

    def barrier(self):
        evs = [self.ev(s) for s in BAR_SEMS]
        for e in ENGINES:
            self.guard[e] = list(evs)

    def op(self, eng, emit, waits=(), inc=None, inc_by=1):
        w = [x for x in waits if x is not None]
        if self.guard[eng]:
            w = self.guard[eng] + w
            self.guard[eng] = []
        if SAFE_MODE:
            w = w + [self.ev(s) for s in BAR_SEMS]
        w = tuple((s, v) for (s, v) in w if v > 0)
        ev = None
        if inc is not None:
            self.cnt[inc] = self.cnt.get(inc, 0) + inc_by
            ev = (inc, self.cnt[inc])
        self.segments[-1][eng].append((w, emit, inc, inc_by))
        return ev

    def dma(self, emit, waits=(), inc=None):
        return self.op("sync", emit, waits, inc, 16)


def build_program():
    nc = bass.Bass("TRN2", target_bir_lowering=False)

    def din(name, shape, dt=F32):
        return nc.dram_tensor(name, list(shape), dt, kind="ExternalInput")

    xT = din("xT", (D, S))
    wqT = din("wqT", (D, D)); wkT = din("wkT", (D, D))
    wvT = din("wvT", (D, D)); woT = din("woT", (D, D))
    gwTd = din("gwT", (D, NE))
    w1d = din("w1d", (NE * D, HD)); w3d = din("w3d", (NE * D, HD))
    w2d = din("w2d", (NE * HD, D))
    sw1d = din("sw1d", (D, HD)); sw3d = din("sw3d", (D, HD))
    sw2d = din("sw2d", (HD, D))
    b1cD = din("b1c_d", (128, NE * NJ)); b3cD = din("b3c_d", (128, NE * NJ))
    sb1cD = din("sb1c_d", (128, NJ)); sb3cD = din("sb3c_d", (128, NJ))
    b2D = din("b2", (NE, D)); sb2D = din("sb2", (1, D))
    bqcD = din("bqc_d", (128, NC8)); bocD = din("boc_d", (128, NC8))
    bkrD = din("bkr", (1, D)); bvrD = din("bvr", (1, D))
    gbrD = din("gbr_d", (1, NE))
    onehotD = din("onehot_d", (NE, D), BF16)
    id128D = din("id128_d", (128, 128), BF16)

    outT = nc.dram_tensor("outT", [D, S], F32, kind="ExternalOutput")
    ikind = "ExternalOutput" if DEBUG_OUTPUTS else "Internal"
    u1T = nc.dram_tensor("u1T", [D, S], F32, kind=ikind)
    u2T = nc.dram_tensor("u2T", [D, S], F32, kind=ikind)
    wbf1 = nc.dram_tensor("wbf1", [NE * 128, NC8 * HD], BF16)
    wbf3 = nc.dram_tensor("wbf3", [NE * 128, NC8 * HD], BF16)
    wbf2 = nc.dram_tensor("wbf2", [NE * 128, NJ * D], BF16)
    swb1 = nc.dram_tensor("swb1", [128, NC8 * HD], BF16)
    swb3 = nc.dram_tensor("swb3", [128, NC8 * HD], BF16)
    swb2 = nc.dram_tensor("swb2", [128, NJ * D], BF16)
    awb = {k: nc.dram_tensor(f"awb_{k}", [128, NC8 * D], BF16)
           for k in ("wq", "wk", "wv", "wo")}

    rowpool = nc.alloc_sbuf_tensor("rowpool", [128, 2048], F32)
    consts = nc.alloc_sbuf_tensor("consts", [128, 1], F32)
    constsb = nc.alloc_sbuf_tensor("constsb", [128, 1], BF16)
    onesR = nc.alloc_sbuf_tensor("onesR", [1, 128], F32)
    onesRb = nc.alloc_sbuf_tensor("onesRb", [1, 512], BF16)
    eps1 = nc.alloc_sbuf_tensor("eps1", [1, 1], F32)
    b1c = nc.alloc_sbuf_tensor("b1c", [128, NE * NJ], F32)
    b3c = nc.alloc_sbuf_tensor("b3c", [128, NE * NJ], F32)
    sb1c = nc.alloc_sbuf_tensor("sb1c", [128, NJ], F32)
    sb3c = nc.alloc_sbuf_tensor("sb3c", [128, NJ], F32)
    b2b = nc.alloc_sbuf_tensor("b2b", [NE, D], BF16)
    sb2b = nc.alloc_sbuf_tensor("sb2b", [1, D], BF16)
    bqc = nc.alloc_sbuf_tensor("bqc", [128, NC8], F32)
    nbqc = nc.alloc_sbuf_tensor("nbqc", [128, NC8], F32)
    boc = nc.alloc_sbuf_tensor("boc", [128, NC8], F32)
    bkrb = nc.alloc_sbuf_tensor("bkrb", [1, D], BF16)
    bvrb = nc.alloc_sbuf_tensor("bvrb", [1, D], BF16)
    gbr = nc.alloc_sbuf_tensor("gbr", [1, NE], F32)
    gbbc = nc.alloc_sbuf_tensor("gbbc", [128, NE], F32)
    gwTb = nc.alloc_sbuf_tensor("gwTb", [128, NC8 * NE], BF16)
    onehot = nc.alloc_sbuf_tensor("onehot", [NE, D], BF16)
    id128 = nc.alloc_sbuf_tensor("id128", [128, 128], BF16)
    gpool = nc.alloc_sbuf_tensor("gpool", [128, 160], F32)
    gpoolb = nc.alloc_sbuf_tensor("gpoolb", [128, 32], BF16)

    R = rowpool.ap()
    r_negm = R[0:1, 0:512]
    r_m2 = R[0:1, 512:1024]
    r_ex2 = R[0:1, 1024:1536]
    r_var = R[0:1, 1536:2048]
    r_lnv = R[0:1, 512:1024]
    r_rstd = R[0:1, 1024:1536]
    r_den = R[0:1, 1536:2048]
    r_rcpb = R[0:1, 0:512].bitcast(BF16)[0:1, 0:512]
    r_bk = R[0:1, 0:1024]
    r_bv = R[0:1, 1024:2048]
    r_sb2 = R[32:33, 0:1024]

    BLOB = 189952
    blob = nc.alloc_sbuf_tensor("blob", [128, BLOB], U8)
    cur = [0]

    def carve(nbytes):
        off = (cur[0] + 63) & ~63
        cur[0] = off + nbytes
        assert cur[0] <= BLOB, f"blob overflow {cur[0]}"
        return off

    def cv(nbytes, dt):
        off = carve(nbytes)
        return blob.ap()[:, off:off + nbytes].bitcast(dt)

    lnA = cv(2 * NC8 * HALF, BF16)
    yac = cv(2 * NC8 * HALF, BF16)
    hbuf = cv(2 * NJ * 1024, BF16)
    w1s = cv(2 * NC8 * HD, BF16)
    w3s = cv(2 * NC8 * HD, BF16)
    w2s = cv(2 * NJ * D, BF16)
    rwfm = cv(2 * 4 * TT, BF16)
    scr_off = carve(4 * 4 * TT)
    def scrF(i):
        return blob.ap()[:, scr_off + i * 2048: scr_off + (i + 1) * 2048].bitcast(F32)
    scr = [scrF(i) for i in range(4)]
    scr_all = blob.ap()[:, scr_off: scr_off + 8192].bitcast(F32)
    xb2 = [blob.ap()[:, scr_off + i * 1024: scr_off + (i + 1) * 1024].bitcast(BF16) for i in range(2)]
    sq2 = [blob.ap()[:, scr_off + 2048 + i * 1024: scr_off + 2048 + (i + 1) * 1024].bitcast(BF16) for i in range(2)]
    stg_off = carve(4 * HD)
    stg = blob.ap()[:, stg_off: stg_off + 4 * HD].bitcast(F32)
    ubuf = [blob.ap()[:, stg_off + i * 2048: stg_off + (i + 1) * 2048].bitcast(F32) for i in range(2)]
    xbuf = cv(4 * NC8 * TT, F32)
    stg2 = xbuf[:, 0:HD]
    stg3 = xbuf[:, HD:2 * HD]
    bounce2 = [xbuf[:, 2 * HD + i * 352:2 * HD + (i + 1) * 352].bitcast(BF16)
               for i in range(2)]
    A_TOTAL = cur[0]

    cur[0] = 0
    qf = cv(2 * NC8 * S, BF16)
    wqs = cv(2 * NC8 * D, BF16)
    wks = cv(2 * NC8 * D, BF16)
    wvs = cv(2 * NC8 * D, BF16)
    ln2 = cv(2 * NC8 * TT, BF16)
    kfb = cv(2 * 4 * D, BF16)
    vtb = cv(2 * 4 * D, BF16)
    u1b = cv(4 * NC8 * TT, F32)
    mrg = cv(2 * NC8 * TT, BF16)
    attb = cv(2 * 2 * TT, BF16)
    kvx = cv(2 * 8 * 65, BF16)
    bscr_off = carve(4 * 4 * TT)
    def bscrF(i):
        return blob.ap()[:, bscr_off + i * 2048: bscr_off + (i + 1) * 2048].bitcast(F32)
    bscr = [bscrF(i) for i in range(4)]
    bxb = [blob.ap()[:, bscr_off + i * 1024: bscr_off + (i + 1) * 1024].bitcast(BF16) for i in range(2)]
    bsq = [blob.ap()[:, bscr_off + 2048 + i * 1024: bscr_off + 2048 + (i + 1) * 1024].bitcast(BF16) for i in range(2)]
    bstg = cv(4 * D, F32)
    obase = cv(2 * 4 * TT, F32)
    obuf = [obase[:, 0:TT], obase[:, TT:2 * TT]]
    bstg2 = obase
    u2b = [cv(4 * TT, F32) for _ in range(2)]
    B_TOTAL = cur[0]
    assert max(A_TOTAL, B_TOTAL) <= BLOB, (A_TOTAL, B_TOTAL)

    pA1a = nc.alloc_psum_tensor("pA1a", [128, 512], F32)
    pA1b = nc.alloc_psum_tensor("pA1b", [128, 512], F32)
    pA3a = nc.alloc_psum_tensor("pA3a", [128, 512], F32)
    pA3b = nc.alloc_psum_tensor("pA3b", [128, 512], F32)
    pRWa = nc.alloc_psum_tensor("pRWa", [128, 512], F32)
    pRWb = nc.alloc_psum_tensor("pRWb", [128, 512], F32)
    pYa = nc.alloc_psum_tensor("pYa", [128, 512], F32)
    pYb = nc.alloc_psum_tensor("pYb", [128, 512], F32)
    PA1 = [pA1a, pA1b]; PA3 = [pA3a, pA3b]; PRW = [pRWa, pRWb]; PY = [pYa, pYb]
    rwT_ps = [pA3a.ap()[0:8, 256:320].bitcast(BF16),
              pA1b.ap()[0:8, 256:320].bitcast(BF16)]

    sch = Sched()
    bank_free = {}

    def bfree(t):
        return list(bank_free.get(t if isinstance(t, (str, tuple)) else t.name, []))

    def bset(t, *evs):
        bank_free[t if isinstance(t, (str, tuple)) else t.name] = [e for e in evs if e]

    def g_init():
        def d(dst, src):
            return sch.dma(lambda s, a=dst, b=src: s.dma_start(a, b), inc="dG")
        for dst, src in [(b1c.ap(), b1cD.ap()), (b3c.ap(), b3cD.ap()),
                         (sb1c.ap(), sb1cD.ap()), (sb3c.ap(), sb3cD.ap()),
                         (bqc.ap(), bqcD.ap()), (boc.ap(), bocD.ap()),
                         (gbr.ap(), gbrD.ap()), (onehot.ap(), onehotD.ap()),
                         (id128.ap(), id128D.ap()),
                         (r_bk, bkrD.ap()), (r_bv, bvrD.ap()), (r_sb2, sb2D.ap()),
                         (scr_all[0:8, 0:1024], b2D.ap()),
                         (scr_all[:, 1024:1088].rearrange("p (c e) -> p c e", e=NE),
                          gwTd.ap().rearrange("(c p) e -> p c e", p=128))]:
            d(dst, src)
        evD = sch.ev("dG")
        sch.op("vector", lambda v: v.memset(consts.ap(), 1.0))
        sch.op("vector", lambda v: v.memset(constsb.ap(), 1.0))
        sch.op("vector", lambda v: v.memset(onesR.ap(), 1.0))
        sch.op("vector", lambda v: v.memset(onesRb.ap(), 1.0))
        evM = sch.op("vector", lambda v: v.memset(eps1.ap(), LN_EPS), inc="cDve")
        sch.op("scalar", lambda sc: sc.activation(b2b.ap(), scr_all[0:8, 0:1024], AF.Identity),
               waits=[evD])
        sch.op("scalar", lambda sc: sc.activation(sb2b.ap(), r_sb2, AF.Identity))
        sch.op("scalar", lambda sc: sc.activation(bkrb.ap(), r_bk, AF.Identity))
        sch.op("scalar", lambda sc: sc.activation(bvrb.ap(), r_bv, AF.Identity))
        sch.op("scalar", lambda sc: sc.activation(gwTb.ap(), scr_all[:, 1024:1088], AF.Identity))
        sch.op("scalar", lambda sc: sc.activation(nbqc.ap(), bqc.ap(), AF.Identity, scale=-1.0),
               inc="cAct")
        evPe = sch.op("tensor", lambda t: t.matmul(pYa.ap()[:, 0:8], onesR.ap(), gbr.ap(),
                                                   start=True, stop=True),
                      waits=[evD, evM], inc="cPe")
        ev2 = sch.op("scalar", lambda sc: sc.activation(gbbc.ap(), pYa.ap()[:, 0:8], AF.Identity),
                     waits=[evPe], inc="cAct")
        bset(pYa, ev2)
        sch.barrier()

    g_init()

    WSLOT = {"w1": w1s, "w3": w3s, "w2": w2s}
    WSEM = {"w1": "dW1", "w3": "dW3", "w2": "dW2"}
    WQUEUE = {"w1": "scalar", "w3": "gpsimd", "w2": "scalar"}
    WSRC = {"w1": w1d, "w3": w3d, "w2": w2d}
    WSRCS = {"w1": sw1d, "w3": sw3d, "w2": sw2d}
    WSRCA = {"wq": wqT, "wk": wkT, "wv": wvT, "wo": woT}
    WCACHE = {"w1": wbf1, "w3": wbf3, "w2": wbf2}
    WCACHES = {"w1": swb1, "w3": swb3, "w2": swb2}
    NPIECE = {"w1": NC8, "w3": NC8, "w2": NJ}
    PIECEW = {"w1": HD, "w3": HD, "w2": D}
    slot_free = {"w1": [], "w3": [], "w2": []}

    stg_last = {}
    cache_ev = {}
    bounce_free = {0: [], 1: []}
    STGSEM = {("w", id(stg)): "dStg", ("w", id(stg2)): "dS2",
              ("w", id(stg3)): "dS3"}

    def qdma(eng, emit, waits=(), inc=None):
        if eng == "sync":
            return sch.dma(emit, waits, inc)
        return sch.op(eng, emit, waits, inc, 16)

    def convert_to_cache(kind, e, bufs):
        attn = kind in WSRCA
        n = NC8 if attn else NPIECE[kind]
        w = D if attn else PIECEW[kind]
        hw_ = w // 2
        if attn:
            src, base, cache = WSRCA[kind], 0, awb[kind]
        else:
            src = WSRCS[kind] if e == NE else WSRC[kind]
            base = 0 if e == NE else e * (D if kind != "w2" else HD)
            cache = WCACHES[kind] if e == NE else WCACHE[kind]

        def crow(pp, h):
            lo, hi = pp * w + h * hw_, pp * w + (h + 1) * hw_
            if attn or e == NE:
                return cache.ap()[:, lo:hi]
            return cache.ap()[e * 128:(e + 1) * 128, lo:hi]

        pend = []
        evw = [None, None]

        def flush():
            pp, b, evd = pend.pop(0)
            key = ("w", id(b))
            evs = []
            for h in range(2):
                bb = bounce2[h]
                evc = sch.op("gpsimd",
                             lambda g, dd=bb[:, 0:hw_], a=b[:, h * hw_:(h + 1) * hw_]:
                             g.tensor_copy(dd, a),
                             waits=[evd] + bounce_free[h], inc="cPool")
                ew = qdma("scalar", lambda s, dd=crow(pp, h), a=bb[:, 0:hw_]:
                          s.dma_start(dd, a), waits=[evc],
                          inc=("dCache" if h == 0 else "dC2"))
                bounce_free[h] = [ew]
                evs.append(evc)
                evw[h] = ew
            stg_last[key] = evs

        for p in range(n):
            b = bufs[p % len(bufs)]
            key = ("w", id(b))
            sem = STGSEM[key]
            sap = src.ap()[base + p * 128: base + p * 128 + 128, :]
            evd = qdma("gpsimd", lambda s, dd=b[:, 0:w], a=sap: s.dma_start(dd, a),
                       waits=stg_last.get(key, []), inc=sem)
            pend.append((p, b, evd))
            if len(pend) >= len(bufs):
                flush()
        while pend:
            flush()
        cache_ev[(kind, e)] = [evw[0], evw[1]]

    def load_weight(kind, e, first, bufs=None, queue=None):
        slot = WSLOT[kind]
        frees = slot_free[kind]
        slot_free[kind] = []
        if not first:
            cache = WCACHES[kind] if e == NE else WCACHE[kind]
            srcap = cache.ap() if e == NE else cache.ap()[e * 128:(e + 1) * 128, :]
            w_ = frees + cache_ev.get((kind, e), [])
            ev = qdma(queue or WQUEUE[kind],
                      lambda s, dd=slot, a=srcap: s.dma_start(dd, a),
                      waits=w_, inc=WSEM[kind])
            return [ev]
        bufs = bufs or [stg]
        n = NPIECE[kind]; w = PIECEW[kind]
        src = WSRCS[kind] if e == NE else WSRC[kind]
        base = 0 if e == NE else e * (D if kind != "w2" else HD)
        pend = []
        evP = [None]

        def flush():
            p, b, evd = pend.pop(0)
            key = ("w", id(b))
            evP[0] = sch.op("vector",
                            lambda v, dd=slot[:, p * w:(p + 1) * w], a=b[:, 0:w]:
                            v.tensor_copy(dd, a),
                            waits=[evd] + (frees if p == 0 else []), inc="cDve")
            stg_last[key] = [evP[0]]

        for p in range(n):
            b = bufs[p % len(bufs)]
            key = ("w", id(b))
            sap = src.ap()[base + p * 128: base + p * 128 + 128, :]
            evd = qdma("gpsimd", lambda s, dd=b[:, 0:w], a=sap: s.dma_start(dd, a),
                       waits=stg_last.get(key, []), inc=STGSEM[key])
            pend.append((p, b, evd))
            if len(pend) >= len(bufs):
                flush()
        while pend:
            flush()
        cache = WCACHES[kind] if e == NE else WCACHE[kind]
        dstap = cache.ap() if e == NE else cache.ap()[e * 128:(e + 1) * 128, :]
        evc = qdma("scalar", lambda s, dd=dstap, a=slot: s.dma_start(dd, a),
                   waits=[evP[0]] + stg_last.get("e0c", []), inc="dG")
        stg_last["e0c"] = [evc]
        cache_ev[(kind, e)] = [evc]
        slot_free[kind] = [evc]
        return [evP[0]]

    ln_state = {"xfree": {}, "statpe": {}, "xcast": {}, "rwT": [], "prev_qf": {}}

    def emit_ln(src_dram, T, ln_dst, chunk_stride, xbuf_, xb_, sq_, tmp2,
                sb_sum, sb_sq, dsem, xkey):
        evX = []
        srcr = src_dram.ap().rearrange("(c p) t -> p c t", p=128)
        for g2 in range(2):
            frees = []
            for c in range(g2 * 4, g2 * 4 + 4):
                frees += ln_state["xfree"].get((xkey, c), [])
            ev = sch.dma(lambda s, dd=xbuf_[:, g2 * 4 * TT:(g2 + 1) * 4 * TT]
                         .rearrange("p (c t) -> p c t", t=TT),
                         a=srcr[:, g2 * 4:(g2 + 1) * 4, T * TT:(T + 1) * TT]:
                         s.dma_start(dd, a),
                         waits=frees, inc=("dX" if g2 == 0 else "dX2"))
            evX += [ev] * 4
        tl = T % 4 if chunk_stride == HALF else 0
        for c in range(NC8):
            par = c % 2
            xc = xbuf_[:, c * TT:(c + 1) * TT]
            pwar = ln_state["statpe"].get((xkey, par), (None, None))
            e1 = sch.op("gpsimd", lambda g, dd=xb_[par], a=xc: g.tensor_copy(dd, a),
                        waits=[evX[c], pwar[0]], inc="cPool")
            e2 = sch.op("scalar", lambda sc, dd=sq_[par], a=xc:
                        sc.activation(dd, a, AF.Square),
                        waits=[evX[c], pwar[1]], inc="cAct")
            mm1 = sch.op("tensor", lambda t, a=xb_[par], c0=(c == 0), c7=(c == NC8 - 1):
                         t.matmul(sb_sum.ap()[0:1, :], constsb.ap(), a, start=c0, stop=c7),
                         waits=[e1] + (bfree(sb_sum) if c == 0 else []), inc="cPe")
            mm2 = sch.op("tensor", lambda t, a=sq_[par], c0=(c == 0), c7=(c == NC8 - 1):
                         t.matmul(sb_sq.ap()[0:1, :], constsb.ap(), a, start=c0, stop=c7),
                         waits=[e2] + (bfree(sb_sq) if c == 0 else []), inc="cPe")
            ln_state["statpe"][(xkey, par)] = (mm1, mm2)
            ln_state["xcast"][(xkey, c)] = [e1, e2]
        evStats = sch.ev("cPe")
        sch.op("scalar", lambda sc: sc.activation(r_negm, sb_sum.ap()[0:1, :], AF.Identity,
                                                  scale=-1.0 / D),
               waits=[evStats], inc="cAct")
        sch.op("scalar", lambda sc: sc.activation(r_m2, r_negm, AF.Square), inc="cAct")
        eA = sch.op("scalar", lambda sc: sc.activation(r_ex2, sb_sq.ap()[0:1, :], AF.Identity,
                                                       scale=1.0 / D), inc="cAct")
        bset(sb_sum, eA); bset(sb_sq, eA)
        eV = sch.op("vector", lambda v: v.tensor_tensor(r_var, r_ex2, r_m2, ALU.subtract),
                    waits=[eA], inc="cDve")
        sch.op("scalar", lambda sc: sc.activation(r_lnv, r_var, AF.Ln, bias=eps1.ap()),
               waits=[eV], inc="cAct")
        eR = sch.op("scalar", lambda sc: sc.activation(r_rstd, r_lnv, AF.Exp, scale=-0.5),
                    inc="cAct")
        eBM = sch.op("tensor", lambda t: t.matmul(pRWa.ap(), onesR.ap(), r_negm,
                                                  start=True, stop=True),
                     waits=[eR] + bfree(pRWa), inc="cPe")
        eBR = sch.op("tensor", lambda t: t.matmul(pRWb.ap(), onesR.ap(), r_rstd,
                                                  start=True, stop=True),
                     waits=bfree(pRWb), inc="cPe")
        lastDve = None
        applies = []
        for c in range(NC8):
            par = c % 2
            xc = xbuf_[:, c * TT:(c + 1) * TT]
            eAdd = sch.op("vector", lambda v, dd=tmp2[par], a=xc:
                          v.tensor_tensor(dd, a, pRWa.ap(), ALU.add),
                          waits=[eBM], inc="cDve")
            dst = ln_dst[:, c * chunk_stride + tl * TT: c * chunk_stride + (tl + 1) * TT]
            lastDve = sch.op("vector", lambda v, dd=dst, a=tmp2[par]:
                             v.tensor_tensor(dd, a, pRWb.ap(), ALU.mult),
                             waits=[eBR], inc="cDve")
            applies.append(lastDve)
            ln_state["xfree"][(xkey, c)] = ln_state["xcast"][(xkey, c)] + [eAdd]
        bset(pRWa, lastDve); bset(pRWb, lastDve)
        return lastDve, applies

    def emit_gate(T, ln_dst, chunk_stride, lastDve):
        tl = T % 4
        rw_ev = None

        def GP(s, a, b):
            return gpool.ap()[:, s * 40 + a: s * 40 + b]
        eGs = []
        for sub in range(4):
            tok0 = tl * TT + sub * 128
            for c in range(NC8):
                eG = sch.op("tensor", lambda t, c_=c, t0=tok0, s_=sub:
                            t.matmul(pA1a.ap()[:, s_ * NE:(s_ + 1) * NE],
                                     ln_dst[:, c_ * chunk_stride + t0: c_ * chunk_stride + t0 + 128],
                                     gwTb.ap()[:, c_ * NE:(c_ + 1) * NE],
                                     start=(c_ == 0), stop=(c_ == NC8 - 1)),
                            waits=([lastDve] + bfree(pA1a)) if (c == 0 and sub == 0) else [],
                            inc=("cPe" if c == NC8 - 1 else None))
            eGs.append(eG)
        ch = [None] * 4
        for sub in range(4):
            ch[sub] = sch.op("scalar",
                             lambda sc, s_=sub: sc.activation(GP(s_, 0, 8),
                                                              pA1a.ap()[:, s_ * NE:(s_ + 1) * NE],
                                                              AF.Exp),
                             waits=[eGs[3], ln_state.get("gate_dve_done")], inc="cAct")
        bset(pA1a, ch[3])
        steps = [
            lambda v, s: v.reduce_sum(GP(s, 32, 33), GP(s, 0, 8), AX.X),
            lambda v, s: v.tensor_scalar(GP(s, 16, 24), gbbc.ap(), GP(s, 32, 33), None, ALU.mult),
            lambda v, s: v.tensor_tensor(GP(s, 16, 24), GP(s, 0, 8), GP(s, 16, 24), ALU.add),
            lambda v, s: v.reduce_max(GP(s, 33, 34), GP(s, 16, 24), AX.X),
            lambda v, s: v.tensor_scalar(GP(s, 24, 32), GP(s, 16, 24), GP(s, 33, 34), None, ALU.subtract),
            lambda v, s: v.tensor_scalar(GP(s, 24, 32), GP(s, 24, 32), 0.0, None, ALU.is_ge),
            lambda v, s: v.scalar_tensor_tensor(GP(s, 24, 32), GP(s, 24, 32), -BIG, GP(s, 16, 24), ALU.mult, ALU.add),
            lambda v, s: v.reduce_max(GP(s, 34, 35), GP(s, 24, 32), AX.X),
            lambda v, s: v.tensor_scalar(GP(s, 24, 32), GP(s, 16, 24), GP(s, 34, 35), None, ALU.is_ge),
            lambda v, s: v.reciprocal(GP(s, 35, 36), GP(s, 32, 33)),
            lambda v, s: v.tensor_tensor(GP(s, 8, 16), GP(s, 0, 8), GP(s, 24, 32), ALU.mult),
            lambda v, s: v.tensor_scalar(gpoolb.ap()[:, s * 8:(s + 1) * 8], GP(s, 8, 16),
                                         GP(s, 35, 36), None, ALU.mult),
        ]
        for k, fn in enumerate(steps):
            last = (k == len(steps) - 1)
            for sub in range(4):
                w = [ch[sub]] + ([ln_state.get(("gtr", sub))] if last else [])
                ch[sub] = sch.op("vector", lambda v, f=fn, s_=sub: f(v, s_),
                                 waits=w, inc="cDve")
        ln_state["gate_dve_done"] = ch[3]
        for sub in range(4):
            tok0 = tl * TT + sub * 128
            sp = sub % 2
            eTr = sch.op("tensor",
                         lambda t, s_=sub, sp_=sp:
                         t.transpose(rwT_ps[sp_], gpoolb.ap()[:, s_ * 8:(s_ + 1) * 8],
                                     id128.ap()),
                         waits=[ch[sub]] + bfree(("rwT", sp))
                               + bfree(pA3a if sp == 0 else pA1b), inc="cPe")
            ln_state[("gtr", sub)] = eTr
            rw_ev = sch.op("scalar",
                           lambda sc, dd=rwfm[0:8, tok0:tok0 + 128], sp_=sp:
                           sc.activation(dd, rwT_ps[sp_], AF.Identity),
                           waits=[eTr], inc="cAct")
            bset(("rwT", sp), rw_ev)
            bset(pA3a if sp == 0 else pA1b, rw_ev)
        return rw_ev

    ep_state = {"ubuf": {}}

    def emit_ffn_phase(src_dram, dst_dram, dsem_out, first0):
        for half in range(2):
            first = first0 and half == 0
            tiles = [half * 4 + i for i in range(4)]
            ln_ready = {}
            ln_applies = {}
            rw_ready = {}
            if not first:
                ready = {k: load_weight(k, 0, first) for k in ("w1", "w3", "w2")}
            pend_g = []
            for T in tiles:
                lastc, applies = emit_ln(src_dram, T, lnA, HALF, xbuf, xb2, sq2,
                                         [scr[2], scr[3]], pYa, pYb, "dX", "A")
                ln_ready[T] = lastc
                ln_applies[T] = applies
                if pend_g:
                    pT, pl_ = pend_g.pop()
                    rw_ready[pT] = emit_gate(pT, lnA, HALF, pl_)
                pend_g.append((T, lastc))
            pT, pl_ = pend_g.pop()
            rw_ready[pT] = emit_gate(pT, lnA, HALF, pl_)
            if first:
                e0b = [stg, hbuf.bitcast(F32)[:, NJ * 512 - HD: NJ * 512]]
                STGSEM[("w", id(e0b[1]))] = "dB"
                ready = {"w1": load_weight("w1", 0, first, e0b),
                         "w3": load_weight("w3", 0, first, e0b),
                         "w2": load_weight("w2", 0, first, [stg])}
                pl = [sch.ev("cPool"), sch.ev("cAct"), sch.ev("cDve")]
                stg_last[("w", id(stg2))] = pl
                stg_last[("w", id(stg3))] = pl
                bounce_free[0] = list(pl)
                bounce_free[1] = list(pl)
                for k in ("w1", "w3", "w2"):
                    convert_to_cache(k, 1, [stg2, stg3])
            scr_war = {0: None, 1: None}
            sil_war = {0: None, 1: None}
            for e in range(NE + 1):
                shared = (e == NE)
                psRW_ev = {}
                ready_n = {}
                for tp in range(2):
                    pair = [tiles[tp * 2], tiles[tp * 2 + 1]]
                    hlast_tile = {}
                    for T in pair:
                        tl = T % 4
                        part = T % 2
                        if not shared:
                            eRWb = sch.op("tensor", lambda t, e_=e, tl_=tl, p_=PRW[part]:
                                          t.matmul(p_.ap(), onehot.ap()[:, e_ * 128:(e_ + 1) * 128],
                                                   rwfm[0:8, tl_ * TT:(tl_ + 1) * TT],
                                                   start=True, stop=True),
                                          waits=[rw_ready[T]] + bfree(PRW[part]), inc="cPe")
                            psRW_ev[T] = (PRW[part], eRWb)
                        for j in range(NJ):
                            jp = j % 2
                            wA = (ready["w1"] + ready["w3"]
                                  + bfree(PA1[jp]) + bfree(PA3[jp]))
                            for c in range(NC8):
                                eA1 = sch.op("tensor", lambda t, c_=c, j_=j, tl_=tl, p_=PA1[jp]:
                                             t.matmul(p_.ap(),
                                                      w1s[:, c_ * HD + j_ * 128: c_ * HD + j_ * 128 + 128],
                                                      lnA[:, c_ * HALF + tl_ * TT: c_ * HALF + (tl_ + 1) * TT],
                                                      start=(c_ == 0), stop=(c_ == NC8 - 1)),
                                             waits=[ln_applies[T][c]] + (wA if c == 0 else []),
                                             inc=("cPe" if c == NC8 - 1 else None))
                            for c in range(NC8):
                                eA3 = sch.op("tensor", lambda t, c_=c, j_=j, tl_=tl, p_=PA3[jp]:
                                             t.matmul(p_.ap(),
                                                      w3s[:, c_ * HD + j_ * 128: c_ * HD + j_ * 128 + 128],
                                                      lnA[:, c_ * HALF + tl_ * TT: c_ * HALF + (tl_ + 1) * TT],
                                                      start=(c_ == 0), stop=(c_ == NC8 - 1)),
                                             waits=[],
                                             inc=("cPe" if c == NC8 - 1 else None))
                            b3col = (sb3c.ap()[:, j:j + 1] if shared
                                     else b3c.ap()[:, e * NJ + j: e * NJ + j + 1])
                            b1col = (sb1c.ap()[:, j:j + 1] if shared
                                     else b1c.ap()[:, e * NJ + j: e * NJ + j + 1])
                            eT3 = sch.op("scalar", lambda sc, p_=PA3[jp], dd=scr[jp], b=b3col:
                                         sc.activation(dd, p_.ap(), AF.Identity, bias=b),
                                         waits=[eA3, scr_war[jp]], inc="cAct")
                            bset(PA3[jp], eT3)
                            eP = sch.op("vector", lambda v, p_=PA1[jp], dd=scr[2 + jp], t3=scr[jp],
                                        b=b1col:
                                        v.scalar_tensor_tensor(dd, p_.ap(), b, t3, ALU.add, ALU.mult),
                                        waits=[eA1, eT3, sil_war[jp]], inc="cDve")
                            bset(PA1[jp], eP)
                            scr_war[jp] = eP
                            hdst = hbuf[:, j * 1024 + (T % 2) * TT: j * 1024 + (T % 2) * TT + TT]
                            if shared:
                                eH = sch.op("scalar", lambda sc, dd=hdst, a=scr[2 + jp]:
                                            sc.activation(dd, a, AF.Silu), waits=[eP], inc="cAct")
                                sil_war[jp] = eH
                            else:
                                eS = sch.op("scalar", lambda sc, dd=scr[2 + jp], a=scr[2 + jp]:
                                            sc.activation(dd, a, AF.Silu), waits=[eP], inc="cAct")
                                eH = sch.op("vector", lambda v, dd=hdst, a=scr[2 + jp],
                                            p_=psRW_ev[T][0]:
                                            v.tensor_tensor(dd, a, p_.ap(), ALU.mult),
                                            waits=[eS, psRW_ev[T][1]], inc="cDve")
                                sil_war[jp] = eH
                                if j == NJ - 1:
                                    bset(psRW_ev[T][0], eH)
                            if j == NJ - 1:
                                hlast_tile[T] = eH
                    eAdone = sch.ev("cPe")
                    if tp == 1 and e < NE:
                        slot_free["w1"].append(eAdone)
                        slot_free["w3"].append(eAdone)
                        if first and e + 2 <= NE:
                            for k in ("w1", "w3", "w2"):
                                convert_to_cache(k, e + 2, [stg2, stg3])
                        elif first and e == NE - 1:
                            for k in ("wq", "wk", "wv", "wo"):
                                convert_to_cache(k, 0, [stg2, stg3])
                        q_ = "scalar" if first else None
                        ready_n["w1"] = load_weight("w1", e + 1, False, queue=q_)
                        ready_n["w3"] = load_weight("w3", e + 1, False, queue=q_)
                    elif tp == 1:
                        slot_free["w1"].append(eAdone)
                        slot_free["w3"].append(eAdone)
                    for T in pair:
                        tl = T % 4
                        for cO in range(NC8):
                            par = cO % 2
                            wts = [hlast_tile[T]] + ready["w2"] + bfree(PY[par])
                            if e == 0:
                                sch.op("tensor", lambda t, cO_=cO, tl_=tl, p_=PY[par]:
                                       t.matmul(p_.ap(), b2b.ap()[:, cO_ * 128:(cO_ + 1) * 128],
                                                rwfm[0:8, tl_ * TT:(tl_ + 1) * TT],
                                                start=True, stop=False),
                                       waits=wts + [rw_ready[T]])
                                sch.op("tensor", lambda t, cO_=cO, p_=PY[par]:
                                       t.matmul(p_.ap(), sb2b.ap()[0:1, cO_ * 128:(cO_ + 1) * 128],
                                                onesRb.ap()[0:1, 0:TT], start=False, stop=False))
                            for j in range(NJ):
                                eY = sch.op("tensor", lambda t, j_=j, cO_=cO, T_=T, p_=PY[par],
                                            s_=(e != 0 and j == 0), sp_=(j == NJ - 1):
                                            t.matmul(p_.ap(),
                                                     w2s[:, j_ * D + cO_ * 128: j_ * D + cO_ * 128 + 128],
                                                     hbuf[:, j_ * 1024 + (T_ % 2) * TT: j_ * 1024 + (T_ % 2) * TT + TT],
                                                     start=s_, stop=sp_),
                                            waits=(wts if (e != 0 and j == 0) else []),
                                            inc=("cPe" if j == NJ - 1 else None))
                            ydst = yac[:, cO * HALF + tl * TT: cO * HALF + (tl + 1) * TT]
                            if e == 0:
                                eYE = sch.op("vector", lambda v, dd=ydst, p_=PY[par]:
                                             v.tensor_copy(dd, p_.ap()), waits=[eY], inc="cDve")
                            else:
                                eYE = sch.op("vector", lambda v, dd=ydst, p_=PY[par]:
                                             v.tensor_tensor(dd, dd, p_.ap(), ALU.add),
                                             waits=[eY], inc="cDve")
                            bset(PY[par], eYE)
                    if tp == 1 and e < NE:
                        slot_free["w2"].append(sch.ev("cPe"))
                        ready_n["w2"] = load_weight("w2", e + 1, False,
                                                    queue=("scalar" if first else None))
                        ready = ready_n
                    elif tp == 1:
                        slot_free["w2"].append(sch.ev("cPe"))
            yac_done = sch.ev("cDve")
            conv_guard = ([sch.ev("cPool"), sch.ev("cDve"), sch.ev("dCache"),
                           sch.ev("dC2")] if first else [])
            srcr = src_dram.ap().rearrange("(c p) t -> p c t", p=128)
            for T in tiles:
                tl = T % 4
                evXg = []
                for g2 in range(2):
                    w_ = [] + (conv_guard if T == tiles[0] else [])
                    for c in range(g2 * 4, g2 * 4 + 4):
                        w_ += ln_state["xfree"].get(("A", c), [])
                    ev = sch.dma(lambda s, dd=xbuf[:, g2 * 4 * TT:(g2 + 1) * 4 * TT]
                                 .rearrange("p (c t) -> p c t", t=TT),
                                 a=srcr[:, g2 * 4:(g2 + 1) * 4, T * TT:(T + 1) * TT]:
                                 s.dma_start(dd, a), waits=w_,
                                 inc=("dX" if g2 == 0 else "dX2"))
                    evXg.append(ev)
                for c in range(NC8):
                    par = c % 2
                    eU = sch.op("vector", lambda v, c_=c, tl_=tl, dd=ubuf[par]:
                                v.scalar_tensor_tensor(
                                    dd, yac[:, c_ * HALF + tl_ * TT: c_ * HALF + (tl_ + 1) * TT],
                                    0.5, xbuf[:, c_ * TT:(c_ + 1) * TT], ALU.mult, ALU.add),
                                waits=[evXg[c // 4], yac_done,
                                       ep_state["ubuf"].get(par)], inc="cDve")
                    eD = qdma("scalar", lambda s, c_=c, T_=T, a=ubuf[par]:
                              s.dma_start(dst_dram.ap()[c_ * 128:(c_ + 1) * 128,
                                                        T_ * TT:(T_ + 1) * TT], a),
                              waits=[eU], inc=("dOut" if par == 0 else "dO2"))
                    ep_state["ubuf"][par] = eD
                    ln_state["xfree"][("A", c)] = [eU]

    ATTSEM = {"wq": "dB", "wk": "dW1", "wv": "dW3", "wo": "dW2"}

    def load_attn_weight(slot, kind, queue, waits=()):
        ev = qdma(queue, lambda s, dd=slot, a=awb[kind].ap(): s.dma_start(dd, a),
                  waits=list(waits) + cache_ev.get((kind, 0), []), inc=ATTSEM[kind])
        return [ev]

    def emit_B1():
        rq = load_attn_weight(wqs, "wq", "scalar")
        rk = load_attn_weight(wks, "wk", "gpsimd")
        rv = load_attn_weight(wvs, "wv", "scalar")
        kv_started = {0: False, 1: False}
        ks_started = {0: False, 1: False}
        elu_war = None
        last_qf = None
        kv_ev = None
        LN2 = [ln2, mrg]
        for T in range(NT):
            lnb = LN2[T % 2]
            lastc, bapp = emit_ln(u1T, T, lnb, TT, u1b, bxb, bsq,
                                  [bstg[:, 0:TT], bstg[:, TT:2 * TT]],
                                  pA1a, pA1b, "dX", "B")
            for i in range(NC8):
                par = i % 2
                wts = rq + bfree(PA1[par])
                for c in range(NC8):
                    eQ = sch.op("tensor", lambda t, c_=c, i_=i, p_=PA1[par], l_=lnb:
                                t.matmul(p_.ap(),
                                         wqs[:, c_ * D + i_ * 128: c_ * D + i_ * 128 + 128],
                                         l_[:, c_ * TT:(c_ + 1) * TT],
                                         start=(c_ == 0), stop=(c_ == NC8 - 1)),
                                waits=[bapp[c]] + (wts if c == 0 else []),
                                inc=("cPe" if c == NC8 - 1 else None))
                e1 = sch.op("vector", lambda v, p_=PA1[par], i_=i:
                            v.tensor_scalar(bscr[2], p_.ap(), bqc.ap()[:, i_:i_ + 1], 0.0,
                                            ALU.add, ALU.max),
                            waits=[eQ, elu_war], inc="cDve")
                e2 = sch.op("scalar", lambda sc, p_=PA1[par], i_=i:
                            sc.activation(bscr[3], p_.ap(), AF.Relu, scale=-1.0,
                                          bias=nbqc.ap()[:, i_:i_ + 1]),
                            waits=[eQ, elu_war, e1], inc="cAct")
                bset(PA1[par], e1, e2)
                e3 = sch.op("scalar", lambda sc: sc.activation(bscr[3], bscr[3], AF.Exp, scale=-1.0),
                            inc="cAct")
                last_qf = sch.op("vector",
                                 lambda v, dd=qf[:, i * S + T * TT: i * S + (T + 1) * TT]:
                                 v.tensor_tensor(dd, bscr[2], bscr[3], ALU.add),
                                 waits=[e3, e1], inc="cDve")
                elu_war = last_qf
            kready = {}
            vready = {}
            for sub in range(4):
                for which in ("k", "v"):
                    slot_w = wks if which == "k" else wvs
                    pb = (PA3 if which == "k" else PRW)[sub % 2]
                    brow = bkrb if which == "k" else bvrb
                    rw_ = rk if which == "k" else rv
                    for i2 in range(2):
                        wts = rw_ + bfree(pb)
                        for c in range(NC8):
                            sch.op("tensor", lambda t, c_=c, i2_=i2, sub_=sub, pp=pb, sw=slot_w, l_=lnb:
                                   t.matmul(pp.ap(),
                                            l_[:, c_ * TT + sub_ * 128: c_ * TT + sub_ * 128 + 128],
                                            sw[:, c_ * D + i2_ * 512: c_ * D + i2_ * 512 + 512],
                                            start=(c_ == 0), stop=False),
                                   waits=[bapp[c]] + (wts if c == 0 else []))
                        eKV = sch.op("tensor", lambda t, i2_=i2, pp=pb, br=brow:
                                     t.matmul(pp.ap(), onesRb.ap()[0:1, 0:128],
                                              br.ap()[0:1, i2_ * 512:(i2_ + 1) * 512],
                                              start=False, stop=True),
                                     inc="cPe")
                        dst = (kfb if which == "k" else vtb)[:, sub * D + i2 * 512: sub * D + i2 * 512 + 512]
                        if which == "k":
                            e1 = sch.op("vector", lambda v, pp=pb:
                                        v.tensor_scalar(bscr[2], pp.ap(), 0.0, None, ALU.max),
                                        waits=[eKV, elu_war], inc="cDve")
                            eb = sch.op("scalar", lambda sc, pp=pb:
                                        sc.activation(bscr[3], pp.ap(), AF.Relu, scale=-1.0),
                                        waits=[eKV, elu_war, e1], inc="cAct")
                            bset(pb, e1, eb)
                            sch.op("scalar", lambda sc: sc.activation(bscr[3], bscr[3], AF.Exp,
                                                                      scale=-1.0), inc="cAct")
                            eE = sch.op("vector", lambda v, dd=dst:
                                        v.tensor_tensor(dd, bscr[2], bscr[3], ALU.add),
                                        waits=[sch.ev("cAct"), e1], inc="cDve")
                            elu_war = eE
                            kready[sub] = eE
                        else:
                            eE = sch.op("scalar", lambda sc, pp=pb, dd=dst:
                                        sc.activation(dd, pp.ap(), AF.Identity),
                                        waits=[eKV], inc="cAct")
                            bset(pb, eE)
                            vready[sub] = eE
            for sub in range(4):
                for h in range(NH):
                    m = h % 2; p = h // 2
                    stv = not kv_started[m]; kv_started[m] = True
                    sch.op("tensor", lambda t, h_=h, m_=m, p_=p, sub_=sub, st=stv:
                           t.matmul(pYa.ap()[m_ * 64:(m_ + 1) * 64, p_ * 64:(p_ + 1) * 64],
                                    kfb[:, sub_ * D + h_ * 64: sub_ * D + h_ * 64 + 64],
                                    vtb[:, sub_ * D + h_ * 64: sub_ * D + h_ * 64 + 64],
                                    start=st, stop=False, skip_group_check=True),
                           waits=[kready[sub], vready[sub]] if h == 0 else [])
                    sts = not ks_started[m]; ks_started[m] = True
                    kv_ev = sch.op("tensor", lambda t, h_=h, m_=m, p_=p, sub_=sub, st=sts:
                                   t.matmul(pYb.ap()[m_ * 64:(m_ + 1) * 64, p_:p_ + 1],
                                            kfb[:, sub_ * D + h_ * 64: sub_ * D + h_ * 64 + 64],
                                            constsb.ap(), start=st, stop=False,
                                            skip_group_check=True),
                                   inc=("cPe" if h == NH - 1 else None))
        eEv = None
        for p in range(8):
            eEv = sch.op("scalar", lambda sc, p_=p:
                         sc.activation(kvx[:, p_ * 65: p_ * 65 + 64],
                                       pYa.ap()[:, p_ * 64:(p_ + 1) * 64], AF.Identity),
                         waits=[kv_ev] if p == 0 else [], inc="cAct")
            eEv = sch.op("scalar", lambda sc, p_=p:
                         sc.activation(kvx[:, p_ * 65 + 64: p_ * 65 + 65],
                                       pYb.ap()[:, p_:p_ + 1], AF.Identity), inc="cAct")
        bset(pYa, eEv); bset(pYb, eEv)
        return last_qf, eEv

    def emit_B2(qf_ev, kvx_ev):
        ro = load_attn_weight(wqs, "wo", "scalar", waits=[qf_ev])
        PS4 = [pA1a, pA1b, pA3a, pA3b]
        attb_war = {}
        obuf_war = {0: None, 1: None}
        u2_war = {}
        prev_wo_pe = None
        prev_bc = None
        rec_war = {}
        rec_war2 = {}
        for T in range(NT):
            bw = []
            for c in range(NC8):
                bw += ln_state["xfree"].get(("B", c), [])
            evX = sch.dma(lambda s, dd=u1b.rearrange("p (c t) -> p c t", t=TT),
                          a=u1T.ap().rearrange("(c p) t -> p c t", p=128)[:, :, T * TT:(T + 1) * TT]:
                          s.dma_start(dd, a), waits=bw, inc="dX")
            mlast = None

            def att_mm(h):
                m = h % 2; p = h // 2
                pb = PS4[h % 4]
                return sch.op("tensor", lambda t, m_=m, p_=p, T_=T, pb_=pb:
                              t.matmul(pb_.ap()[0:65, :],
                                       kvx[m_ * 64:(m_ + 1) * 64, p_ * 65: p_ * 65 + 65],
                                       qf[m_ * 64:(m_ + 1) * 64, p_ * S + T_ * TT: p_ * S + (T_ + 1) * TT],
                                       start=True, stop=True),
                              waits=[kvx_ev, qf_ev] + bfree(pb), inc="cPe")
            attE = {h: att_mm(h) for h in range(4)}
            for h in range(NH):
                m = h % 2; p = h // 2
                pb = PS4[h % 4]
                eAtt = attE[h]
                e1 = sch.op("vector", lambda v, pb_=pb:
                            v.tensor_scalar(r_rcpb, pb_.ap()[64:65, :], ATT_EPS, None, ALU.max),
                            waits=[eAtt, prev_bc], inc="cDve")
                slot = (m, (p % 2))
                eEv = sch.op("scalar", lambda sc, pb_=pb, m_=m, p_=p:
                             sc.activation(attb[m_ * 64:(m_ + 1) * 64,
                                                (p_ % 2) * TT:(p_ % 2) * TT + TT],
                                           pb_.ap()[0:64, :], AF.Identity),
                             waits=[eAtt, attb_war.get(slot), e1], inc="cAct")
                eBc = sch.op("tensor", lambda t, p_=PRW[h % 2]:
                             t.matmul(p_.ap()[0:64, :], onesRb.ap()[0:1, 0:64], r_rcpb,
                                      start=True, stop=True),
                             waits=[e1] + bfree(PRW[h % 2]), inc="cPe")
                prev_bc = eBc
                bset(pb, eEv, e1)
                if h + 4 < NH:
                    attE[h + 4] = att_mm(h + 4)
                rsl = (h % 2)
                recf = bscr[2 + rsl]
                recb = kfb[m * 64:(m + 1) * 64, rsl * TT: rsl * TT + TT]
                eL = sch.op("scalar", lambda sc, pr=PRW[h % 2], dd=recf:
                            sc.activation(dd[0:64, :], pr.ap()[0:64, :], AF.Ln),
                            waits=[eBc, rec_war.get(rsl)], inc="cAct")
                eR = sch.op("scalar", lambda sc, dd=recb, a=recf:
                            sc.activation(dd, a[0:64, :], AF.Exp, scale=-1.0),
                            waits=[rec_war2.get((m, rsl))], inc="cAct")
                bset(PRW[h % 2], eL)
                eM = sch.op("vector", lambda v, m_=m, p_=p, a=recb:
                            v.tensor_tensor(mrg[m_ * 64:(m_ + 1) * 64, p_ * TT:(p_ + 1) * TT],
                                            attb[m_ * 64:(m_ + 1) * 64, (p_ % 2) * TT:(p_ % 2) * TT + TT],
                                            a, ALU.mult),
                            waits=[eEv, eR, prev_wo_pe], inc="cDve")
                prev_wo_pe = None
                rec_war[rsl] = eR
                rec_war2[(m, rsl)] = eM
                attb_war[slot] = eM
                mlast = eM
            for cO in range(NC8):
                par = cO % 2
                wts = [mlast, evX] + ro + bfree(PY[par])
                for cI in range(NC8):
                    eO = sch.op("tensor", lambda t, cI_=cI, cO_=cO, p_=PY[par]:
                                t.matmul(p_.ap(),
                                         wqs[:, cI_ * D + cO_ * 128: cI_ * D + cO_ * 128 + 128],
                                         mrg[:, cI_ * TT:(cI_ + 1) * TT],
                                         start=(cI_ == 0), stop=(cI_ == NC8 - 1)),
                                waits=wts if cI == 0 else [],
                                inc=("cPe" if cI == NC8 - 1 else None))
                eOb = sch.op("scalar", lambda sc, p_=PY[par], cO_=cO, dd=obuf[par]:
                             sc.activation(dd, p_.ap(), AF.Identity, bias=boc.ap()[:, cO_:cO_ + 1]),
                             waits=[eO, obuf_war.get(par)], inc="cAct")
                bset(PY[par], eOb)
                eU2 = sch.op("vector", lambda v, dd=u2b[par], a=obuf[par], c_=cO:
                             v.tensor_tensor(dd, u1b[:, c_ * TT:(c_ + 1) * TT], a, ALU.add),
                             waits=[eOb, u2_war.get(par)], inc="cDve")
                obuf_war[par] = eU2
                eD = qdma("scalar", lambda s, c_=cO, T_=T, a=u2b[par]:
                          s.dma_start(u2T.ap()[c_ * 128:(c_ + 1) * 128, T_ * TT:(T_ + 1) * TT], a),
                          waits=[eU2], inc=("dOut" if par == 0 else "dO2"))
                u2_war[par] = eD
            prev_wo_pe = sch.ev("cPe")
            dv = sch.ev("cDve")
            for c in range(NC8):
                ln_state["xfree"][("B", c)] = [dv]

    if DEBUG_DUMPS:
        dbg_ln = nc.dram_tensor("dbg_ln", [128, NC8 * HALF], BF16, kind="ExternalOutput")
        dbg_y = nc.dram_tensor("dbg_y", [128, NC8 * HALF], BF16, kind="ExternalOutput")
        dbg_rw = nc.dram_tensor("dbg_rw", [128, 4 * TT], BF16, kind="ExternalOutput")
        dbg_g = nc.dram_tensor("dbg_g", [128, 160], F32, kind="ExternalOutput")
        dbg_gb = nc.dram_tensor("dbg_gb", [128, 32], BF16, kind="ExternalOutput")
    if "A" in PHASES:
        emit_ffn_phase(xT, u1T, "dX", True)
        if DEBUG_DUMPS:
            wts = [sch.ev("cPe"), sch.ev("cAct"), sch.ev("cDve")]
            sch.dma(lambda s: s.dma_start(dbg_ln.ap(), lnA), waits=wts, inc="dOut")
            sch.dma(lambda s: s.dma_start(dbg_y.ap(), yac), inc="dOut")
            sch.dma(lambda s: s.dma_start(dbg_rw.ap(), rwfm), inc="dOut")
            sch.dma(lambda s: s.dma_start(dbg_g.ap(), gpool.ap()), inc="dOut")
            sch.dma(lambda s: s.dma_start(dbg_gb.ap(), gpoolb.ap()), inc="dOut")
    sch.barrier()
    sch.new_segment()
    qf_ev = kvx_ev = None
    if "B1" in PHASES:
        qf_ev, kvx_ev = emit_B1()
    if "B2" in PHASES:
        emit_B2(qf_ev, kvx_ev)
    sch.barrier()
    sch.new_segment()
    if "C" in PHASES:
        emit_ffn_phase(u2T, outT, "dOut", False)
    sch.op("sync", lambda s: s.nop(), waits=[sch.ev("dOut"), sch.ev("dO2")])

    hw = {e: {} for e in ENGINES}
    with ExitStack() as st:
        semh = {n: st.enter_context(nc.semaphore(n)) for n in SEM_NAMES}

        def mk(engname, seg):
            def run(eng):
                for waits, emit, inc, inc_by in seg[engname]:
                    for (s_, v_) in waits:
                        if v_ > hw[engname].get(s_, 0):
                            eng.wait_ge(semh[s_], v_)
                            hw[engname][s_] = v_
                    inst = emit(eng)
                    if inc is not None:
                        inst.then_inc(semh[inc], inc_by)
            return run

        for i, seg in enumerate(sch.segments):
            with nc.Block(f"seg{i}") as blk:
                blk.sync(mk("sync", seg))
                blk.tensor(mk("tensor", seg))
                blk.vector(mk("vector", seg))
                blk.scalar(mk("scalar", seg))
                blk.gpsimd(mk("gpsimd", seg))
    return nc


def _prep_inputs(inputs):
    f32 = np.float32
    bf16 = ml_dtypes.bfloat16
    g = lambda k: np.asarray(inputs[k], f32)
    wq, bq, wk, bk = g("L0_attn_wq"), g("L0_attn_bq"), g("L0_attn_wk"), g("L0_attn_bk")
    wv, bv, wo, bo = g("L0_attn_wv"), g("L0_attn_bv"), g("L0_attn_wo"), g("L0_attn_bo")
    gw, gb = g("L0_ffn_gate_w"), g("L0_ffn_gate_b")
    w1, b1, w3, b3 = g("L0_ffn_w1"), g("L0_ffn_b1"), g("L0_ffn_w3"), g("L0_ffn_b3")
    w2, b2 = g("L0_ffn_w2"), g("L0_ffn_b2")
    sw1, sb1, sw3, sb3 = g("L0_ffn_sw1"), g("L0_ffn_sb1"), g("L0_ffn_sw3"), g("L0_ffn_sb3")
    sw2, sb2 = g("L0_ffn_sw2"), g("L0_ffn_sb2")
    onehot = np.zeros((NE, D), bf16)
    for e in range(NE):
        onehot[e, e * 128:(e + 1) * 128] = 1
    common = {
        "wqT": np.ascontiguousarray(wq.T), "wkT": np.ascontiguousarray(wk.T),
        "wvT": np.ascontiguousarray(wv.T), "woT": np.ascontiguousarray(wo.T),
        "gwT": np.ascontiguousarray(gw.T),
        "w1d": np.ascontiguousarray(w1.reshape(NE * D, HD)),
        "w3d": np.ascontiguousarray(w3.reshape(NE * D, HD)),
        "w2d": np.ascontiguousarray(w2.reshape(NE * HD, D)),
        "sw1d": sw1, "sw3d": sw3, "sw2d": sw2,
        "b1c_d": np.ascontiguousarray(b1.reshape(NE, NJ, 128).transpose(2, 0, 1).reshape(128, NE * NJ)),
        "b3c_d": np.ascontiguousarray(b3.reshape(NE, NJ, 128).transpose(2, 0, 1).reshape(128, NE * NJ)),
        "sb1c_d": np.ascontiguousarray(sb1.reshape(NJ, 128).T),
        "sb3c_d": np.ascontiguousarray(sb3.reshape(NJ, 128).T),
        "b2": b2, "sb2": sb2.reshape(1, D),
        "bqc_d": np.ascontiguousarray(bq.reshape(NC8, 128).T),
        "boc_d": np.ascontiguousarray(bo.reshape(NC8, 128).T),
        "bkr": bk.reshape(1, D), "bvr": bv.reshape(1, D),
        "gbr_d": gb.reshape(1, NE),
        "onehot_d": onehot, "id128_d": np.eye(128, dtype=bf16),
    }
    x = np.asarray(inputs["L0_attn_x"], f32)
    in_maps = []
    for c in range(NCORES):
        m = dict(common)
        m["xT"] = np.ascontiguousarray(x[c].T)
        in_maps.append(m)
    return in_maps


def kernel(**inputs):
    in_maps = _prep_inputs(inputs)
    nc = build_program()
    res = run_bass_kernel_spmd(nc, in_maps, core_ids=list(range(NCORES)))
    out = np.empty((NCORES, S, D), np.float32)
    for c in range(NCORES):
        out[c] = res.results[c]["outT"].T
    return out


# version 10
# speedup vs baseline: 1.2121x; 1.2121x over previous
import numpy as np
import ml_dtypes
import concourse.bass as bass
import concourse.mybir as mybir
from concourse.bass_utils import run_bass_kernel_spmd
from contextlib import ExitStack

F32 = mybir.dt.float32
BF16 = mybir.dt.bfloat16
U8 = mybir.dt.uint8
AF = mybir.ActivationFunctionType
ALU = mybir.AluOpType
AX = mybir.AxisListType

S = 4096
D = 1024
HD = 1408
NE = 8
NC8 = 8
NJ = 11
NH = 16
TT = 512
NT = S // TT
HALF = 2048
LN_EPS = 1e-5
ATT_EPS = 1e-6
BIG = 1e30
NCORES = 8

DEBUG_OUTPUTS = False
DEBUG_DUMPS = False
PHASES = ("A", "B1", "B2", "C")
SAFE_MODE = False

SEM_NAMES = ["dG", "dX", "dX2", "dStg", "dS2", "dS3", "dW1", "dW3", "dW2",
             "dCache", "dC2", "dOut", "dO2", "dB", "cPe", "cAct", "cDve", "cPool"]
ENGINES = ("sync", "tensor", "vector", "scalar", "gpsimd")
BAR_SEMS = ["cPe", "cAct", "cDve", "cPool", "dX", "dX2", "dStg", "dS2", "dS3",
            "dW1", "dW3", "dW2", "dCache", "dC2", "dOut", "dO2", "dB", "dG"]


class Sched:
    def __init__(self):
        self.segments = [{e: [] for e in ENGINES}]
        self.cnt = {}
        self.guard = {e: [] for e in ENGINES}

    def new_segment(self):
        self.segments.append({e: [] for e in ENGINES})

    def ev(self, sem):
        return (sem, self.cnt.get(sem, 0))

    def barrier(self):
        evs = [self.ev(s) for s in BAR_SEMS]
        for e in ENGINES:
            self.guard[e] = list(evs)

    def op(self, eng, emit, waits=(), inc=None, inc_by=1):
        w = [x for x in waits if x is not None]
        if self.guard[eng]:
            w = self.guard[eng] + w
            self.guard[eng] = []
        if SAFE_MODE:
            w = w + [self.ev(s) for s in BAR_SEMS]
        w = tuple((s, v) for (s, v) in w if v > 0)
        ev = None
        if inc is not None:
            self.cnt[inc] = self.cnt.get(inc, 0) + inc_by
            ev = (inc, self.cnt[inc])
        self.segments[-1][eng].append((w, emit, inc, inc_by))
        return ev

    def dma(self, emit, waits=(), inc=None):
        return self.op("sync", emit, waits, inc, 16)


def build_program():
    nc = bass.Bass("TRN2", target_bir_lowering=False)

    def din(name, shape, dt=F32):
        return nc.dram_tensor(name, list(shape), dt, kind="ExternalInput")

    xT = din("xT", (D, S))
    wqT = din("wqT", (D, D)); wkT = din("wkT", (D, D))
    wvT = din("wvT", (D, D)); woT = din("woT", (D, D))
    gwTd = din("gwT", (D, NE))
    w1d = din("w1d", (NE * D, HD)); w3d = din("w3d", (NE * D, HD))
    w2d = din("w2d", (NE * HD, D))
    sw1d = din("sw1d", (D, HD)); sw3d = din("sw3d", (D, HD))
    sw2d = din("sw2d", (HD, D))
    b1cD = din("b1c_d", (128, NE * NJ)); b3cD = din("b3c_d", (128, NE * NJ))
    sb1cD = din("sb1c_d", (128, NJ)); sb3cD = din("sb3c_d", (128, NJ))
    b2D = din("b2", (NE, D)); sb2D = din("sb2", (1, D))
    bqcD = din("bqc_d", (128, NC8)); bocD = din("boc_d", (128, NC8))
    bkrD = din("bkr", (1, D)); bvrD = din("bvr", (1, D))
    gbrD = din("gbr_d", (1, NE))
    onehotD = din("onehot_d", (NE, D), BF16)
    id128D = din("id128_d", (128, 128), BF16)

    outT = nc.dram_tensor("outT", [D, S], F32, kind="ExternalOutput")
    ikind = "ExternalOutput" if DEBUG_OUTPUTS else "Internal"
    u1T = nc.dram_tensor("u1T", [D, S], F32, kind=ikind)
    u2T = nc.dram_tensor("u2T", [D, S], F32, kind=ikind)
    wbf1 = nc.dram_tensor("wbf1", [NE * 128, NC8 * HD], BF16)
    wbf3 = nc.dram_tensor("wbf3", [NE * 128, NC8 * HD], BF16)
    wbf2 = nc.dram_tensor("wbf2", [NE * 128, NJ * D], BF16)
    swb1 = nc.dram_tensor("swb1", [128, NC8 * HD], BF16)
    swb3 = nc.dram_tensor("swb3", [128, NC8 * HD], BF16)
    swb2 = nc.dram_tensor("swb2", [128, NJ * D], BF16)
    awb = {k: nc.dram_tensor(f"awb_{k}", [128, NC8 * D], BF16)
           for k in ("wq", "wk", "wv", "wo")}

    rowpool = nc.alloc_sbuf_tensor("rowpool", [128, 2048], F32)
    consts = nc.alloc_sbuf_tensor("consts", [128, 1], F32)
    constsb = nc.alloc_sbuf_tensor("constsb", [128, 1], BF16)
    onesR = nc.alloc_sbuf_tensor("onesR", [1, 128], F32)
    onesRb = nc.alloc_sbuf_tensor("onesRb", [1, 512], BF16)
    eps1 = nc.alloc_sbuf_tensor("eps1", [1, 1], F32)
    b1c = nc.alloc_sbuf_tensor("b1c", [128, NE * NJ], F32)
    b3c = nc.alloc_sbuf_tensor("b3c", [128, NE * NJ], F32)
    sb1c = nc.alloc_sbuf_tensor("sb1c", [128, NJ], F32)
    sb3c = nc.alloc_sbuf_tensor("sb3c", [128, NJ], F32)
    b2b = nc.alloc_sbuf_tensor("b2b", [NE, D], BF16)
    sb2b = nc.alloc_sbuf_tensor("sb2b", [1, D], BF16)
    bqc = nc.alloc_sbuf_tensor("bqc", [128, NC8], F32)
    nbqc = nc.alloc_sbuf_tensor("nbqc", [128, NC8], F32)
    boc = nc.alloc_sbuf_tensor("boc", [128, NC8], F32)
    bkrb = nc.alloc_sbuf_tensor("bkrb", [1, D], BF16)
    bvrb = nc.alloc_sbuf_tensor("bvrb", [1, D], BF16)
    gbr = nc.alloc_sbuf_tensor("gbr", [1, NE], F32)
    gbbc = nc.alloc_sbuf_tensor("gbbc", [128, NE], F32)
    gwTb = nc.alloc_sbuf_tensor("gwTb", [128, NC8 * NE], BF16)
    onehot = nc.alloc_sbuf_tensor("onehot", [NE, D], BF16)
    id128 = nc.alloc_sbuf_tensor("id128", [128, 128], BF16)
    gpool = nc.alloc_sbuf_tensor("gpool", [128, 160], F32)
    gpoolb = nc.alloc_sbuf_tensor("gpoolb", [128, 32], BF16)

    R = rowpool.ap()
    r_negm = R[0:1, 0:512]
    r_m2 = R[0:1, 512:1024]
    r_ex2 = R[0:1, 1024:1536]
    r_var = R[0:1, 1536:2048]
    r_lnv = R[0:1, 512:1024]
    r_rstd = R[0:1, 1024:1536]
    r_den = R[0:1, 1536:2048]
    r_rcpb = R[0:1, 0:512].bitcast(BF16)[0:1, 0:512]
    r_bk = R[0:1, 0:1024]
    r_bv = R[0:1, 1024:2048]
    r_sb2 = R[32:33, 0:1024]

    BLOB = 189952
    blob = nc.alloc_sbuf_tensor("blob", [128, BLOB], U8)
    cur = [0]

    def carve(nbytes):
        off = (cur[0] + 63) & ~63
        cur[0] = off + nbytes
        assert cur[0] <= BLOB, f"blob overflow {cur[0]}"
        return off

    def cv(nbytes, dt):
        off = carve(nbytes)
        return blob.ap()[:, off:off + nbytes].bitcast(dt)

    lnA = cv(2 * NC8 * HALF, BF16)
    yac = cv(2 * NC8 * HALF, BF16)
    hbuf = cv(2 * NJ * 1024, BF16)
    w1s = cv(2 * NC8 * HD, BF16)
    w3s = cv(2 * NC8 * HD, BF16)
    w2s = cv(2 * NJ * D, BF16)
    rwfm = cv(2 * 4 * TT, BF16)
    scr_off = carve(4 * 4 * TT)
    def scrF(i):
        return blob.ap()[:, scr_off + i * 2048: scr_off + (i + 1) * 2048].bitcast(F32)
    scr = [scrF(i) for i in range(4)]
    scr_all = blob.ap()[:, scr_off: scr_off + 8192].bitcast(F32)
    xb2 = [blob.ap()[:, scr_off + i * 1024: scr_off + (i + 1) * 1024].bitcast(BF16) for i in range(2)]
    sq2 = [blob.ap()[:, scr_off + 2048 + i * 1024: scr_off + 2048 + (i + 1) * 1024].bitcast(BF16) for i in range(2)]
    stg_off = carve(4 * HD)
    stg = blob.ap()[:, stg_off: stg_off + 4 * HD].bitcast(F32)
    ubuf = [blob.ap()[:, stg_off + i * 2048: stg_off + (i + 1) * 2048].bitcast(F32) for i in range(2)]
    xbuf = cv(4 * NC8 * TT, F32)
    stg2 = xbuf[:, 0:HD]
    stg3 = xbuf[:, HD:2 * HD]
    bounce2 = [xbuf[:, 2 * HD + i * 352:2 * HD + (i + 1) * 352].bitcast(BF16)
               for i in range(2)]
    A_TOTAL = cur[0]

    cur[0] = 0
    qf = cv(2 * NC8 * S, BF16)
    wqs = cv(2 * NC8 * D, BF16)
    wks = cv(2 * NC8 * D, BF16)
    wvs = cv(2 * NC8 * D, BF16)
    ln2 = cv(2 * NC8 * TT, BF16)
    kfb = cv(2 * 4 * D, BF16)
    vtb = cv(2 * 4 * D, BF16)
    u1b = cv(4 * NC8 * TT, F32)
    mrg = cv(2 * NC8 * TT, BF16)
    attb = cv(2 * 2 * TT, BF16)
    kvx = cv(2 * 8 * 65, BF16)
    bscr_off = carve(4 * 4 * TT)
    def bscrF(i):
        return blob.ap()[:, bscr_off + i * 2048: bscr_off + (i + 1) * 2048].bitcast(F32)
    bscr = [bscrF(i) for i in range(4)]
    bxb = [blob.ap()[:, bscr_off + i * 1024: bscr_off + (i + 1) * 1024].bitcast(BF16) for i in range(2)]
    bsq = [blob.ap()[:, bscr_off + 2048 + i * 1024: bscr_off + 2048 + (i + 1) * 1024].bitcast(BF16) for i in range(2)]
    bstg = cv(4 * D, F32)
    obase = cv(2 * 4 * TT, F32)
    obuf = [obase[:, 0:TT], obase[:, TT:2 * TT]]
    bstg2 = obase
    u2b = [cv(4 * TT, F32) for _ in range(2)]
    B_TOTAL = cur[0]
    assert max(A_TOTAL, B_TOTAL) <= BLOB, (A_TOTAL, B_TOTAL)

    pA1a = nc.alloc_psum_tensor("pA1a", [128, 512], F32)
    pA1b = nc.alloc_psum_tensor("pA1b", [128, 512], F32)
    pA3a = nc.alloc_psum_tensor("pA3a", [128, 512], F32)
    pA3b = nc.alloc_psum_tensor("pA3b", [128, 512], F32)
    pRWa = nc.alloc_psum_tensor("pRWa", [128, 512], F32)
    pRWb = nc.alloc_psum_tensor("pRWb", [128, 512], F32)
    pYa = nc.alloc_psum_tensor("pYa", [128, 512], F32)
    pYb = nc.alloc_psum_tensor("pYb", [128, 512], F32)
    PA1 = [pA1a, pA1b]; PA3 = [pA3a, pA3b]; PRW = [pRWa, pRWb]; PY = [pYa, pYb]
    rwT_ps = [pA3a.ap()[0:8, 256:320].bitcast(BF16),
              pA1b.ap()[0:8, 256:320].bitcast(BF16)]

    sch = Sched()
    bank_free = {}

    def bfree(t):
        return list(bank_free.get(t if isinstance(t, (str, tuple)) else t.name, []))

    def bset(t, *evs):
        bank_free[t if isinstance(t, (str, tuple)) else t.name] = [e for e in evs if e]

    def g_init():
        def d(dst, src):
            return sch.dma(lambda s, a=dst, b=src: s.dma_start(a, b), inc="dG")
        for dst, src in [(b1c.ap(), b1cD.ap()), (b3c.ap(), b3cD.ap()),
                         (sb1c.ap(), sb1cD.ap()), (sb3c.ap(), sb3cD.ap()),
                         (bqc.ap(), bqcD.ap()), (boc.ap(), bocD.ap()),
                         (gbr.ap(), gbrD.ap()), (onehot.ap(), onehotD.ap()),
                         (id128.ap(), id128D.ap()),
                         (r_bk, bkrD.ap()), (r_bv, bvrD.ap()), (r_sb2, sb2D.ap()),
                         (scr_all[0:8, 0:1024], b2D.ap()),
                         (scr_all[:, 1024:1088].rearrange("p (c e) -> p c e", e=NE),
                          gwTd.ap().rearrange("(c p) e -> p c e", p=128))]:
            d(dst, src)
        evD = sch.ev("dG")
        sch.op("vector", lambda v: v.memset(consts.ap(), 1.0))
        sch.op("vector", lambda v: v.memset(constsb.ap(), 1.0))
        sch.op("vector", lambda v: v.memset(onesR.ap(), 1.0))
        sch.op("vector", lambda v: v.memset(onesRb.ap(), 1.0))
        evM = sch.op("vector", lambda v: v.memset(eps1.ap(), LN_EPS), inc="cDve")
        sch.op("scalar", lambda sc: sc.activation(b2b.ap(), scr_all[0:8, 0:1024], AF.Identity),
               waits=[evD])
        sch.op("scalar", lambda sc: sc.activation(sb2b.ap(), r_sb2, AF.Identity))
        sch.op("scalar", lambda sc: sc.activation(bkrb.ap(), r_bk, AF.Identity))
        sch.op("scalar", lambda sc: sc.activation(bvrb.ap(), r_bv, AF.Identity))
        sch.op("scalar", lambda sc: sc.activation(gwTb.ap(), scr_all[:, 1024:1088], AF.Identity))
        sch.op("scalar", lambda sc: sc.activation(nbqc.ap(), bqc.ap(), AF.Identity, scale=-1.0),
               inc="cAct")
        evPe = sch.op("tensor", lambda t: t.matmul(pYa.ap()[:, 0:8], onesR.ap(), gbr.ap(),
                                                   start=True, stop=True),
                      waits=[evD, evM], inc="cPe")
        ev2 = sch.op("scalar", lambda sc: sc.activation(gbbc.ap(), pYa.ap()[:, 0:8], AF.Identity),
                     waits=[evPe], inc="cAct")
        bset(pYa, ev2)
        sch.barrier()

    g_init()

    WSLOT = {"w1": w1s, "w3": w3s, "w2": w2s}
    WSEM = {"w1": "dW1", "w3": "dW3", "w2": "dW2"}
    WQUEUE = {"w1": "scalar", "w3": "gpsimd", "w2": "scalar"}
    WSRC = {"w1": w1d, "w3": w3d, "w2": w2d}
    WSRCS = {"w1": sw1d, "w3": sw3d, "w2": sw2d}
    WSRCA = {"wq": wqT, "wk": wkT, "wv": wvT, "wo": woT}
    WCACHE = {"w1": wbf1, "w3": wbf3, "w2": wbf2}
    WCACHES = {"w1": swb1, "w3": swb3, "w2": swb2}
    NPIECE = {"w1": NC8, "w3": NC8, "w2": NJ}
    PIECEW = {"w1": HD, "w3": HD, "w2": D}
    slot_free = {"w1": [], "w3": [], "w2": []}

    stg_last = {}
    cache_ev = {}
    bounce_free = {0: [], 1: []}
    STGSEM = {("w", id(stg)): "dStg", ("w", id(stg2)): "dS2",
              ("w", id(stg3)): "dS3"}

    def qdma(eng, emit, waits=(), inc=None):
        if eng == "sync":
            return sch.dma(emit, waits, inc)
        return sch.op(eng, emit, waits, inc, 16)

    def convert_to_cache(kind, e, bufs):
        attn = kind in WSRCA
        n = NC8 if attn else NPIECE[kind]
        w = D if attn else PIECEW[kind]
        hw_ = w // 2
        if attn:
            src, base, cache = WSRCA[kind], 0, awb[kind]
        else:
            src = WSRCS[kind] if e == NE else WSRC[kind]
            base = 0 if e == NE else e * (D if kind != "w2" else HD)
            cache = WCACHES[kind] if e == NE else WCACHE[kind]

        def crow(pp, h):
            lo, hi = pp * w + h * hw_, pp * w + (h + 1) * hw_
            if attn or e == NE:
                return cache.ap()[:, lo:hi]
            return cache.ap()[e * 128:(e + 1) * 128, lo:hi]

        pend = []
        evw = [None, None]

        def flush():
            pp, b, evd = pend.pop(0)
            key = ("w", id(b))
            evs = []
            for h in range(2):
                bb = bounce2[h]
                evc = sch.op("gpsimd",
                             lambda g, dd=bb[:, 0:hw_], a=b[:, h * hw_:(h + 1) * hw_]:
                             g.tensor_copy(dd, a),
                             waits=[evd] + bounce_free[h], inc="cPool")
                ew = qdma("sync", lambda s, dd=crow(pp, h), a=bb[:, 0:hw_]:
                          s.dma_start(dd, a), waits=[evc],
                          inc=("dCache" if h == 0 else "dC2"))
                bounce_free[h] = [ew]
                evs.append(evc)
                evw[h] = ew
            stg_last[key] = evs

        for p in range(n):
            b = bufs[p % len(bufs)]
            key = ("w", id(b))
            sem = STGSEM[key]
            sap = src.ap()[base + p * 128: base + p * 128 + 128, :]
            evd = qdma("gpsimd", lambda s, dd=b[:, 0:w], a=sap: s.dma_start(dd, a),
                       waits=stg_last.get(key, []), inc=sem)
            pend.append((p, b, evd))
            if len(pend) >= len(bufs):
                flush()
        while pend:
            flush()
        cache_ev[(kind, e)] = [evw[0], evw[1]]

    def load_weight(kind, e, first, bufs=None, queue=None):
        slot = WSLOT[kind]
        frees = slot_free[kind]
        slot_free[kind] = []
        if not first:
            cache = WCACHES[kind] if e == NE else WCACHE[kind]
            srcap = cache.ap() if e == NE else cache.ap()[e * 128:(e + 1) * 128, :]
            w_ = frees + cache_ev.get((kind, e), [])
            ev = qdma(queue or WQUEUE[kind],
                      lambda s, dd=slot, a=srcap: s.dma_start(dd, a),
                      waits=w_, inc=WSEM[kind])
            return [ev]
        bufs = bufs or [stg]
        n = NPIECE[kind]; w = PIECEW[kind]
        src = WSRCS[kind] if e == NE else WSRC[kind]
        base = 0 if e == NE else e * (D if kind != "w2" else HD)
        pend = []
        evP = [None]

        def flush():
            p, b, evd = pend.pop(0)
            key = ("w", id(b))
            evP[0] = sch.op("vector",
                            lambda v, dd=slot[:, p * w:(p + 1) * w], a=b[:, 0:w]:
                            v.tensor_copy(dd, a),
                            waits=[evd] + (frees if p == 0 else []), inc="cDve")
            stg_last[key] = [evP[0]]

        for p in range(n):
            b = bufs[p % len(bufs)]
            key = ("w", id(b))
            sap = src.ap()[base + p * 128: base + p * 128 + 128, :]
            evd = qdma("gpsimd", lambda s, dd=b[:, 0:w], a=sap: s.dma_start(dd, a),
                       waits=stg_last.get(key, []), inc=STGSEM[key])
            pend.append((p, b, evd))
            if len(pend) >= len(bufs):
                flush()
        while pend:
            flush()
        cache = WCACHES[kind] if e == NE else WCACHE[kind]
        dstap = cache.ap() if e == NE else cache.ap()[e * 128:(e + 1) * 128, :]
        evc = qdma("sync", lambda s, dd=dstap, a=slot: s.dma_start(dd, a),
                   waits=[evP[0]] + stg_last.get("e0c", []), inc="dG")
        stg_last["e0c"] = [evc]
        cache_ev[(kind, e)] = [evc]
        slot_free[kind] = [evc]
        return [evP[0]]

    ln_state = {"xfree": {}, "statpe": {}, "xcast": {}, "rwT": [], "prev_qf": {}}

    def emit_ln(src_dram, T, ln_dst, chunk_stride, xbuf_, xb_, sq_, tmp2,
                sb_sum, sb_sq, dsem, xkey):
        evX = []
        srcr = src_dram.ap().rearrange("(c p) t -> p c t", p=128)
        for g2 in range(2):
            frees = []
            for c in range(g2 * 4, g2 * 4 + 4):
                frees += ln_state["xfree"].get((xkey, c), [])
            ev = sch.dma(lambda s, dd=xbuf_[:, g2 * 4 * TT:(g2 + 1) * 4 * TT]
                         .rearrange("p (c t) -> p c t", t=TT),
                         a=srcr[:, g2 * 4:(g2 + 1) * 4, T * TT:(T + 1) * TT]:
                         s.dma_start(dd, a),
                         waits=frees, inc=("dX" if g2 == 0 else "dX2"))
            evX += [ev] * 4
        tl = T % 4 if chunk_stride == HALF else 0
        for c in range(NC8):
            par = c % 2
            xc = xbuf_[:, c * TT:(c + 1) * TT]
            pwar = ln_state["statpe"].get((xkey, par), (None, None))
            e1 = sch.op("gpsimd", lambda g, dd=xb_[par], a=xc: g.tensor_copy(dd, a),
                        waits=[evX[c], pwar[0]], inc="cPool")
            e2 = sch.op("scalar", lambda sc, dd=sq_[par], a=xc:
                        sc.activation(dd, a, AF.Square),
                        waits=[evX[c], pwar[1]], inc="cAct")
            mm1 = sch.op("tensor", lambda t, a=xb_[par], c0=(c == 0), c7=(c == NC8 - 1):
                         t.matmul(sb_sum.ap()[0:1, :], constsb.ap(), a, start=c0, stop=c7),
                         waits=[e1] + (bfree(sb_sum) if c == 0 else []), inc="cPe")
            mm2 = sch.op("tensor", lambda t, a=sq_[par], c0=(c == 0), c7=(c == NC8 - 1):
                         t.matmul(sb_sq.ap()[0:1, :], constsb.ap(), a, start=c0, stop=c7),
                         waits=[e2] + (bfree(sb_sq) if c == 0 else []), inc="cPe")
            ln_state["statpe"][(xkey, par)] = (mm1, mm2)
            ln_state["xcast"][(xkey, c)] = [e1, e2]
        evStats = sch.ev("cPe")
        sch.op("scalar", lambda sc: sc.activation(r_negm, sb_sum.ap()[0:1, :], AF.Identity,
                                                  scale=-1.0 / D),
               waits=[evStats], inc="cAct")
        sch.op("scalar", lambda sc: sc.activation(r_m2, r_negm, AF.Square), inc="cAct")
        eA = sch.op("scalar", lambda sc: sc.activation(r_ex2, sb_sq.ap()[0:1, :], AF.Identity,
                                                       scale=1.0 / D), inc="cAct")
        bset(sb_sum, eA); bset(sb_sq, eA)
        eV = sch.op("vector", lambda v: v.tensor_tensor(r_var, r_ex2, r_m2, ALU.subtract),
                    waits=[eA], inc="cDve")
        sch.op("scalar", lambda sc: sc.activation(r_lnv, r_var, AF.Ln, bias=eps1.ap()),
               waits=[eV], inc="cAct")
        eR = sch.op("scalar", lambda sc: sc.activation(r_rstd, r_lnv, AF.Exp, scale=-0.5),
                    inc="cAct")
        eBM = sch.op("tensor", lambda t: t.matmul(pRWa.ap(), onesR.ap(), r_negm,
                                                  start=True, stop=True),
                     waits=[eR] + bfree(pRWa), inc="cPe")
        eBR = sch.op("tensor", lambda t: t.matmul(pRWb.ap(), onesR.ap(), r_rstd,
                                                  start=True, stop=True),
                     waits=bfree(pRWb), inc="cPe")
        lastDve = None
        applies = []
        for c in range(NC8):
            par = c % 2
            xc = xbuf_[:, c * TT:(c + 1) * TT]
            eAdd = sch.op("vector", lambda v, dd=tmp2[par], a=xc:
                          v.tensor_tensor(dd, a, pRWa.ap(), ALU.add),
                          waits=[eBM], inc="cDve")
            dst = ln_dst[:, c * chunk_stride + tl * TT: c * chunk_stride + (tl + 1) * TT]
            lastDve = sch.op("vector", lambda v, dd=dst, a=tmp2[par]:
                             v.tensor_tensor(dd, a, pRWb.ap(), ALU.mult),
                             waits=[eBR], inc="cDve")
            applies.append(lastDve)
            ln_state["xfree"][(xkey, c)] = ln_state["xcast"][(xkey, c)] + [eAdd]
        bset(pRWa, lastDve); bset(pRWb, lastDve)
        return lastDve, applies

    def emit_gate(T, ln_dst, chunk_stride, lastDve):
        tl = T % 4
        rw_ev = None

        def GP(s, a, b):
            return gpool.ap()[:, s * 40 + a: s * 40 + b]
        eGs = []
        for sub in range(4):
            tok0 = tl * TT + sub * 128
            for c in range(NC8):
                eG = sch.op("tensor", lambda t, c_=c, t0=tok0, s_=sub:
                            t.matmul(pA1a.ap()[:, s_ * NE:(s_ + 1) * NE],
                                     ln_dst[:, c_ * chunk_stride + t0: c_ * chunk_stride + t0 + 128],
                                     gwTb.ap()[:, c_ * NE:(c_ + 1) * NE],
                                     start=(c_ == 0), stop=(c_ == NC8 - 1)),
                            waits=([lastDve] + bfree(pA1a)) if (c == 0 and sub == 0) else [],
                            inc=("cPe" if c == NC8 - 1 else None))
            eGs.append(eG)
        ch = [None] * 4
        for sub in range(4):
            ch[sub] = sch.op("scalar",
                             lambda sc, s_=sub: sc.activation(GP(s_, 0, 8),
                                                              pA1a.ap()[:, s_ * NE:(s_ + 1) * NE],
                                                              AF.Exp),
                             waits=[eGs[3], ln_state.get("gate_dve_done")], inc="cAct")
        bset(pA1a, ch[3])
        steps = [
            lambda v, s: v.reduce_sum(GP(s, 32, 33), GP(s, 0, 8), AX.X),
            lambda v, s: v.tensor_scalar(GP(s, 16, 24), gbbc.ap(), GP(s, 32, 33), None, ALU.mult),
            lambda v, s: v.tensor_tensor(GP(s, 16, 24), GP(s, 0, 8), GP(s, 16, 24), ALU.add),
            lambda v, s: v.reduce_max(GP(s, 33, 34), GP(s, 16, 24), AX.X),
            lambda v, s: v.tensor_scalar(GP(s, 24, 32), GP(s, 16, 24), GP(s, 33, 34), None, ALU.subtract),
            lambda v, s: v.tensor_scalar(GP(s, 24, 32), GP(s, 24, 32), 0.0, None, ALU.is_ge),
            lambda v, s: v.scalar_tensor_tensor(GP(s, 24, 32), GP(s, 24, 32), -BIG, GP(s, 16, 24), ALU.mult, ALU.add),
            lambda v, s: v.reduce_max(GP(s, 34, 35), GP(s, 24, 32), AX.X),
            lambda v, s: v.tensor_scalar(GP(s, 24, 32), GP(s, 16, 24), GP(s, 34, 35), None, ALU.is_ge),
            lambda v, s: v.reciprocal(GP(s, 35, 36), GP(s, 32, 33)),
            lambda v, s: v.tensor_tensor(GP(s, 8, 16), GP(s, 0, 8), GP(s, 24, 32), ALU.mult),
            lambda v, s: v.tensor_scalar(gpoolb.ap()[:, s * 8:(s + 1) * 8], GP(s, 8, 16),
                                         GP(s, 35, 36), None, ALU.mult),
        ]
        for k, fn in enumerate(steps):
            last = (k == len(steps) - 1)
            for sub in range(4):
                w = [ch[sub]] + ([ln_state.get(("gtr", sub))] if last else [])
                ch[sub] = sch.op("vector", lambda v, f=fn, s_=sub: f(v, s_),
                                 waits=w, inc="cDve")
        ln_state["gate_dve_done"] = ch[3]
        for sub in range(4):
            tok0 = tl * TT + sub * 128
            sp = sub % 2
            eTr = sch.op("tensor",
                         lambda t, s_=sub, sp_=sp:
                         t.transpose(rwT_ps[sp_], gpoolb.ap()[:, s_ * 8:(s_ + 1) * 8],
                                     id128.ap()),
                         waits=[ch[sub]] + bfree(("rwT", sp))
                               + bfree(pA3a if sp == 0 else pA1b), inc="cPe")
            ln_state[("gtr", sub)] = eTr
            rw_ev = sch.op("scalar",
                           lambda sc, dd=rwfm[0:8, tok0:tok0 + 128], sp_=sp:
                           sc.activation(dd, rwT_ps[sp_], AF.Identity),
                           waits=[eTr], inc="cAct")
            bset(("rwT", sp), rw_ev)
            bset(pA3a if sp == 0 else pA1b, rw_ev)
        return rw_ev

    ep_state = {"ubuf": {}}

    def emit_ffn_phase(src_dram, dst_dram, dsem_out, first0):
        for half in range(2):
            first = first0 and half == 0
            tiles = [half * 4 + i for i in range(4)]
            ln_ready = {}
            ln_applies = {}
            rw_ready = {}
            if not first:
                ready = {k: load_weight(k, 0, first) for k in ("w1", "w3", "w2")}
            pend_g = []
            for T in tiles:
                lastc, applies = emit_ln(src_dram, T, lnA, HALF, xbuf, xb2, sq2,
                                         [scr[2], scr[3]], pYa, pYb, "dX", "A")
                ln_ready[T] = lastc
                ln_applies[T] = applies
                if pend_g:
                    pT, pl_ = pend_g.pop()
                    rw_ready[pT] = emit_gate(pT, lnA, HALF, pl_)
                pend_g.append((T, lastc))
            pT, pl_ = pend_g.pop()
            rw_ready[pT] = emit_gate(pT, lnA, HALF, pl_)
            if first:
                e0b = [stg, hbuf.bitcast(F32)[:, NJ * 512 - HD: NJ * 512]]
                STGSEM[("w", id(e0b[1]))] = "dB"
                ready = {"w1": load_weight("w1", 0, first, e0b),
                         "w3": load_weight("w3", 0, first, e0b),
                         "w2": load_weight("w2", 0, first, [stg])}
                pl = [sch.ev("cPool"), sch.ev("cAct"), sch.ev("cDve")]
                stg_last[("w", id(stg2))] = pl
                stg_last[("w", id(stg3))] = pl
                bounce_free[0] = list(pl)
                bounce_free[1] = list(pl)
                for k in ("w1", "w3", "w2"):
                    convert_to_cache(k, 1, [stg2, stg3])
            scr_war = {0: None, 1: None}
            sil_war = {0: None, 1: None}
            epx = {}
            srcr = src_dram.ap().rearrange("(c p) t -> p c t", p=128)

            def ep_xload(T, extra):
                evs = []
                for g2 in range(2):
                    w_ = list(extra)
                    for c in range(g2 * 4, g2 * 4 + 4):
                        w_ += ln_state["xfree"].get(("A", c), [])
                    ev = sch.dma(lambda s, dd=xbuf[:, g2 * 4 * TT:(g2 + 1) * 4 * TT]
                                 .rearrange("p (c t) -> p c t", t=TT),
                                 a=srcr[:, g2 * 4:(g2 + 1) * 4, T * TT:(T + 1) * TT]:
                                 s.dma_start(dd, a), waits=w_,
                                 inc=("dX" if g2 == 0 else "dX2"))
                    evs.append(ev)
                return evs

            for e in range(NE + 1):
                shared = (e == NE)
                psRW_ev = {}
                ready_n = {}
                for tp in range(2):
                    pair = [tiles[tp * 2], tiles[tp * 2 + 1]]
                    hlast_tile = {}
                    if shared:
                        cg = ([sch.ev("cPool"), sch.ev("cDve"), sch.ev("dCache"),
                               sch.ev("dC2")] if (first and tp == 0) else [])
                        epx[pair[0]] = ep_xload(pair[0], cg)
                    for T in pair:
                        tl = T % 4
                        part = T % 2
                        if not shared:
                            eRWb = sch.op("tensor", lambda t, e_=e, tl_=tl, p_=PRW[part]:
                                          t.matmul(p_.ap(), onehot.ap()[:, e_ * 128:(e_ + 1) * 128],
                                                   rwfm[0:8, tl_ * TT:(tl_ + 1) * TT],
                                                   start=True, stop=True),
                                          waits=[rw_ready[T]] + bfree(PRW[part]), inc="cPe")
                            psRW_ev[T] = (PRW[part], eRWb)
                        for j in range(NJ):
                            jp = j % 2
                            wA = (ready["w1"] + ready["w3"]
                                  + bfree(PA1[jp]) + bfree(PA3[jp]))
                            for c in range(NC8):
                                eA1 = sch.op("tensor", lambda t, c_=c, j_=j, tl_=tl, p_=PA1[jp]:
                                             t.matmul(p_.ap(),
                                                      w1s[:, c_ * HD + j_ * 128: c_ * HD + j_ * 128 + 128],
                                                      lnA[:, c_ * HALF + tl_ * TT: c_ * HALF + (tl_ + 1) * TT],
                                                      start=(c_ == 0), stop=(c_ == NC8 - 1)),
                                             waits=[ln_applies[T][c]] + (wA if c == 0 else []),
                                             inc=("cPe" if c == NC8 - 1 else None))
                            for c in range(NC8):
                                eA3 = sch.op("tensor", lambda t, c_=c, j_=j, tl_=tl, p_=PA3[jp]:
                                             t.matmul(p_.ap(),
                                                      w3s[:, c_ * HD + j_ * 128: c_ * HD + j_ * 128 + 128],
                                                      lnA[:, c_ * HALF + tl_ * TT: c_ * HALF + (tl_ + 1) * TT],
                                                      start=(c_ == 0), stop=(c_ == NC8 - 1)),
                                             waits=[],
                                             inc=("cPe" if c == NC8 - 1 else None))
                            b3col = (sb3c.ap()[:, j:j + 1] if shared
                                     else b3c.ap()[:, e * NJ + j: e * NJ + j + 1])
                            b1col = (sb1c.ap()[:, j:j + 1] if shared
                                     else b1c.ap()[:, e * NJ + j: e * NJ + j + 1])
                            eT3 = sch.op("scalar", lambda sc, p_=PA3[jp], dd=scr[jp], b=b3col:
                                         sc.activation(dd, p_.ap(), AF.Identity, bias=b),
                                         waits=[eA3, scr_war[jp]], inc="cAct")
                            bset(PA3[jp], eT3)
                            eP = sch.op("vector", lambda v, p_=PA1[jp], dd=scr[2 + jp], t3=scr[jp],
                                        b=b1col:
                                        v.scalar_tensor_tensor(dd, p_.ap(), b, t3, ALU.add, ALU.mult),
                                        waits=[eA1, eT3, sil_war[jp]], inc="cDve")
                            bset(PA1[jp], eP)
                            scr_war[jp] = eP
                            hdst = hbuf[:, j * 1024 + (T % 2) * TT: j * 1024 + (T % 2) * TT + TT]
                            if shared:
                                eH = sch.op("scalar", lambda sc, dd=hdst, a=scr[2 + jp]:
                                            sc.activation(dd, a, AF.Silu), waits=[eP], inc="cAct")
                                sil_war[jp] = eH
                            else:
                                eS = sch.op("scalar", lambda sc, dd=scr[2 + jp], a=scr[2 + jp]:
                                            sc.activation(dd, a, AF.Silu), waits=[eP], inc="cAct")
                                eH = sch.op("vector", lambda v, dd=hdst, a=scr[2 + jp],
                                            p_=psRW_ev[T][0]:
                                            v.tensor_tensor(dd, a, p_.ap(), ALU.mult),
                                            waits=[eS, psRW_ev[T][1]], inc="cDve")
                                sil_war[jp] = eH
                                if j == NJ - 1:
                                    bset(psRW_ev[T][0], eH)
                            if j == NJ - 1:
                                hlast_tile[T] = eH
                    eAdone = sch.ev("cPe")
                    if tp == 1 and e < NE:
                        slot_free["w1"].append(eAdone)
                        slot_free["w3"].append(eAdone)
                        if first and e + 2 <= NE:
                            for k in ("w1", "w3", "w2"):
                                convert_to_cache(k, e + 2, [stg2, stg3])
                        elif first and e == NE - 1:
                            for k in ("wq", "wk", "wv", "wo"):
                                convert_to_cache(k, 0, [stg2, stg3])
                        q_ = "sync" if first else None
                        ready_n["w1"] = load_weight("w1", e + 1, False, queue=q_)
                        ready_n["w3"] = load_weight("w3", e + 1, False, queue=q_)
                    elif tp == 1:
                        slot_free["w1"].append(eAdone)
                        slot_free["w3"].append(eAdone)
                    for T in pair:
                        tl = T % 4
                        if shared and T == pair[1]:
                            epx[T] = ep_xload(T, [])
                        for cO in range(NC8):
                            par = cO % 2
                            wts = [hlast_tile[T]] + ready["w2"] + bfree(PY[par])
                            if e == 0:
                                sch.op("tensor", lambda t, cO_=cO, tl_=tl, p_=PY[par]:
                                       t.matmul(p_.ap(), b2b.ap()[:, cO_ * 128:(cO_ + 1) * 128],
                                                rwfm[0:8, tl_ * TT:(tl_ + 1) * TT],
                                                start=True, stop=False),
                                       waits=wts + [rw_ready[T]])
                                sch.op("tensor", lambda t, cO_=cO, p_=PY[par]:
                                       t.matmul(p_.ap(), sb2b.ap()[0:1, cO_ * 128:(cO_ + 1) * 128],
                                                onesRb.ap()[0:1, 0:TT], start=False, stop=False))
                            for j in range(NJ):
                                eY = sch.op("tensor", lambda t, j_=j, cO_=cO, T_=T, p_=PY[par],
                                            s_=(e != 0 and j == 0), sp_=(j == NJ - 1):
                                            t.matmul(p_.ap(),
                                                     w2s[:, j_ * D + cO_ * 128: j_ * D + cO_ * 128 + 128],
                                                     hbuf[:, j_ * 1024 + (T_ % 2) * TT: j_ * 1024 + (T_ % 2) * TT + TT],
                                                     start=s_, stop=sp_),
                                            waits=(wts if (e != 0 and j == 0) else []),
                                            inc=("cPe" if j == NJ - 1 else None))
                            ydst = yac[:, cO * HALF + tl * TT: cO * HALF + (tl + 1) * TT]
                            if e == 0:
                                eYE = sch.op("vector", lambda v, dd=ydst, p_=PY[par]:
                                             v.tensor_copy(dd, p_.ap()), waits=[eY], inc="cDve")
                            else:
                                eYE = sch.op("vector", lambda v, dd=ydst, p_=PY[par]:
                                             v.tensor_tensor(dd, dd, p_.ap(), ALU.add),
                                             waits=[eY], inc="cDve")
                            bset(PY[par], eYE)
                            if shared:
                                eU = sch.op("vector", lambda v, c_=cO, tl_=tl, dd=ubuf[par]:
                                            v.scalar_tensor_tensor(
                                                dd, yac[:, c_ * HALF + tl_ * TT: c_ * HALF + (tl_ + 1) * TT],
                                                0.5, xbuf[:, c_ * TT:(c_ + 1) * TT],
                                                ALU.mult, ALU.add),
                                            waits=[epx[T][cO // 4], eYE,
                                                   ep_state["ubuf"].get(par)], inc="cDve")
                                eD = qdma("scalar", lambda s, c_=cO, T_=T, a=ubuf[par]:
                                          s.dma_start(dst_dram.ap()[c_ * 128:(c_ + 1) * 128,
                                                                    T_ * TT:(T_ + 1) * TT], a),
                                          waits=[eU], inc=("dOut" if par == 0 else "dO2"))
                                ep_state["ubuf"][par] = eD
                                ln_state["xfree"][("A", cO)] = [eU]
                    if tp == 1 and e < NE:
                        slot_free["w2"].append(sch.ev("cPe"))
                        ready_n["w2"] = load_weight("w2", e + 1, False,
                                                    queue=("sync" if first else None))
                        ready = ready_n
                    elif tp == 1:
                        slot_free["w2"].append(sch.ev("cPe"))

    ATTSEM = {"wq": "dB", "wk": "dW1", "wv": "dW3", "wo": "dW2"}

    def load_attn_weight(slot, kind, queue, waits=()):
        ev = qdma(queue, lambda s, dd=slot, a=awb[kind].ap(): s.dma_start(dd, a),
                  waits=list(waits) + cache_ev.get((kind, 0), []), inc=ATTSEM[kind])
        return [ev]

    def emit_B1():
        rq = load_attn_weight(wqs, "wq", "scalar")
        rk = load_attn_weight(wks, "wk", "gpsimd")
        rv = load_attn_weight(wvs, "wv", "scalar")
        kv_started = {0: False, 1: False}
        ks_started = {0: False, 1: False}
        elu_war = None
        last_qf = None
        kv_ev = None
        LN2 = [ln2, mrg]
        pend_b = [None]

        def attn_tile(T, lnb, bapp):
            nonlocal elu_war, last_qf, kv_ev
            for i in range(NC8):
                par = i % 2
                wts = rq + bfree(PA1[par])
                for c in range(NC8):
                    eQ = sch.op("tensor", lambda t, c_=c, i_=i, p_=PA1[par], l_=lnb:
                                t.matmul(p_.ap(),
                                         wqs[:, c_ * D + i_ * 128: c_ * D + i_ * 128 + 128],
                                         l_[:, c_ * TT:(c_ + 1) * TT],
                                         start=(c_ == 0), stop=(c_ == NC8 - 1)),
                                waits=[bapp[c]] + (wts if c == 0 else []),
                                inc=("cPe" if c == NC8 - 1 else None))
                e1 = sch.op("vector", lambda v, p_=PA1[par], i_=i:
                            v.tensor_scalar(bscr[2], p_.ap(), bqc.ap()[:, i_:i_ + 1], 0.0,
                                            ALU.add, ALU.max),
                            waits=[eQ, elu_war], inc="cDve")
                e2 = sch.op("scalar", lambda sc, p_=PA1[par], i_=i:
                            sc.activation(bscr[3], p_.ap(), AF.Relu, scale=-1.0,
                                          bias=nbqc.ap()[:, i_:i_ + 1]),
                            waits=[eQ, elu_war, e1], inc="cAct")
                bset(PA1[par], e1, e2)
                e3 = sch.op("scalar", lambda sc: sc.activation(bscr[3], bscr[3], AF.Exp, scale=-1.0),
                            inc="cAct")
                last_qf = sch.op("vector",
                                 lambda v, dd=qf[:, i * S + T * TT: i * S + (T + 1) * TT]:
                                 v.tensor_tensor(dd, bscr[2], bscr[3], ALU.add),
                                 waits=[e3, e1], inc="cDve")
                elu_war = last_qf
            kready = {}
            vready = {}
            for sub in range(4):
                for which in ("k", "v"):
                    slot_w = wks if which == "k" else wvs
                    pb = (PA3 if which == "k" else PRW)[sub % 2]
                    brow = bkrb if which == "k" else bvrb
                    rw_ = rk if which == "k" else rv
                    for i2 in range(2):
                        wts = rw_ + bfree(pb)
                        for c in range(NC8):
                            sch.op("tensor", lambda t, c_=c, i2_=i2, sub_=sub, pp=pb, sw=slot_w, l_=lnb:
                                   t.matmul(pp.ap(),
                                            l_[:, c_ * TT + sub_ * 128: c_ * TT + sub_ * 128 + 128],
                                            sw[:, c_ * D + i2_ * 512: c_ * D + i2_ * 512 + 512],
                                            start=(c_ == 0), stop=False),
                                   waits=[bapp[c]] + (wts if c == 0 else []))
                        eKV = sch.op("tensor", lambda t, i2_=i2, pp=pb, br=brow:
                                     t.matmul(pp.ap(), onesRb.ap()[0:1, 0:128],
                                              br.ap()[0:1, i2_ * 512:(i2_ + 1) * 512],
                                              start=False, stop=True),
                                     inc="cPe")
                        dst = (kfb if which == "k" else vtb)[:, sub * D + i2 * 512: sub * D + i2 * 512 + 512]
                        if which == "k":
                            e1 = sch.op("vector", lambda v, pp=pb:
                                        v.tensor_scalar(bscr[2], pp.ap(), 0.0, None, ALU.max),
                                        waits=[eKV, elu_war], inc="cDve")
                            eb = sch.op("scalar", lambda sc, pp=pb:
                                        sc.activation(bscr[3], pp.ap(), AF.Relu, scale=-1.0),
                                        waits=[eKV, elu_war, e1], inc="cAct")
                            bset(pb, e1, eb)
                            sch.op("scalar", lambda sc: sc.activation(bscr[3], bscr[3], AF.Exp,
                                                                      scale=-1.0), inc="cAct")
                            eE = sch.op("vector", lambda v, dd=dst:
                                        v.tensor_tensor(dd, bscr[2], bscr[3], ALU.add),
                                        waits=[sch.ev("cAct"), e1], inc="cDve")
                            elu_war = eE
                            kready[sub] = eE
                        else:
                            eE = sch.op("scalar", lambda sc, pp=pb, dd=dst:
                                        sc.activation(dd, pp.ap(), AF.Identity),
                                        waits=[eKV], inc="cAct")
                            bset(pb, eE)
                            vready[sub] = eE
            for sub in range(4):
                for h in range(NH):
                    m = h % 2; p = h // 2
                    stv = not kv_started[m]; kv_started[m] = True
                    sch.op("tensor", lambda t, h_=h, m_=m, p_=p, sub_=sub, st=stv:
                           t.matmul(pYa.ap()[m_ * 64:(m_ + 1) * 64, p_ * 64:(p_ + 1) * 64],
                                    kfb[:, sub_ * D + h_ * 64: sub_ * D + h_ * 64 + 64],
                                    vtb[:, sub_ * D + h_ * 64: sub_ * D + h_ * 64 + 64],
                                    start=st, stop=False, skip_group_check=True),
                           waits=[kready[sub], vready[sub]] if h == 0 else [])
                    sts = not ks_started[m]; ks_started[m] = True
                    kv_ev = sch.op("tensor", lambda t, h_=h, m_=m, p_=p, sub_=sub, st=sts:
                                   t.matmul(pYb.ap()[m_ * 64:(m_ + 1) * 64, p_:p_ + 1],
                                            kfb[:, sub_ * D + h_ * 64: sub_ * D + h_ * 64 + 64],
                                            constsb.ap(), start=st, stop=False,
                                            skip_group_check=True),
                                   inc=("cPe" if h == NH - 1 else None))

        for T in range(NT):
            lnb = LN2[T % 2]
            lastc, bapp = emit_ln(u1T, T, lnb, TT, u1b, bxb, bsq,
                                  [bstg[:, 0:TT], bstg[:, TT:2 * TT]],
                                  pA1a, pA1b, "dX", "B")
            if pend_b[0] is not None:
                attn_tile(*pend_b[0])
            pend_b[0] = (T, lnb, bapp)
        attn_tile(*pend_b[0])
        eEv = None
        for p in range(8):
            eEv = sch.op("scalar", lambda sc, p_=p:
                         sc.activation(kvx[:, p_ * 65: p_ * 65 + 64],
                                       pYa.ap()[:, p_ * 64:(p_ + 1) * 64], AF.Identity),
                         waits=[kv_ev] if p == 0 else [], inc="cAct")
            eEv = sch.op("scalar", lambda sc, p_=p:
                         sc.activation(kvx[:, p_ * 65 + 64: p_ * 65 + 65],
                                       pYb.ap()[:, p_:p_ + 1], AF.Identity), inc="cAct")
        bset(pYa, eEv); bset(pYb, eEv)
        return last_qf, eEv

    def emit_B2(qf_ev, kvx_ev):
        ro = load_attn_weight(wqs, "wo", "scalar", waits=[qf_ev])
        PS4 = [pA1a, pA1b, pA3a, pA3b]
        attb_war = {}
        obuf_war = {0: None, 1: None}
        u2_war = {}
        prev_wo_pe = None
        prev_bc = None
        rec_war = {}
        rec_war2 = {}
        for T in range(NT):
            bw = []
            for c in range(NC8):
                bw += ln_state["xfree"].get(("B", c), [])
            evX = sch.dma(lambda s, dd=u1b.rearrange("p (c t) -> p c t", t=TT),
                          a=u1T.ap().rearrange("(c p) t -> p c t", p=128)[:, :, T * TT:(T + 1) * TT]:
                          s.dma_start(dd, a), waits=bw, inc="dX")
            mlast = None

            def att_mm(h):
                m = h % 2; p = h // 2
                pb = PS4[h % 4]
                return sch.op("tensor", lambda t, m_=m, p_=p, T_=T, pb_=pb:
                              t.matmul(pb_.ap()[0:65, :],
                                       kvx[m_ * 64:(m_ + 1) * 64, p_ * 65: p_ * 65 + 65],
                                       qf[m_ * 64:(m_ + 1) * 64, p_ * S + T_ * TT: p_ * S + (T_ + 1) * TT],
                                       start=True, stop=True),
                              waits=[kvx_ev, qf_ev] + bfree(pb), inc="cPe")
            attE = {h: att_mm(h) for h in range(4)}
            for h in range(NH):
                m = h % 2; p = h // 2
                pb = PS4[h % 4]
                eAtt = attE[h]
                e1 = sch.op("vector", lambda v, pb_=pb:
                            v.tensor_scalar(r_rcpb, pb_.ap()[64:65, :], ATT_EPS, None, ALU.max),
                            waits=[eAtt, prev_bc], inc="cDve")
                slot = (m, (p % 2))
                eEv = sch.op("scalar", lambda sc, pb_=pb, m_=m, p_=p:
                             sc.activation(attb[m_ * 64:(m_ + 1) * 64,
                                                (p_ % 2) * TT:(p_ % 2) * TT + TT],
                                           pb_.ap()[0:64, :], AF.Identity),
                             waits=[eAtt, attb_war.get(slot), e1], inc="cAct")
                eBc = sch.op("tensor", lambda t, p_=PRW[h % 2]:
                             t.matmul(p_.ap()[0:64, :], onesRb.ap()[0:1, 0:64], r_rcpb,
                                      start=True, stop=True),
                             waits=[e1] + bfree(PRW[h % 2]), inc="cPe")
                prev_bc = eBc
                bset(pb, eEv, e1)
                if h + 4 < NH:
                    attE[h + 4] = att_mm(h + 4)
                rsl = (h % 2)
                recf = bscr[2 + rsl]
                recb = kfb[m * 64:(m + 1) * 64, rsl * TT: rsl * TT + TT]
                eL = sch.op("scalar", lambda sc, pr=PRW[h % 2], dd=recf:
                            sc.activation(dd[0:64, :], pr.ap()[0:64, :], AF.Ln),
                            waits=[eBc, rec_war.get(rsl)], inc="cAct")
                eR = sch.op("scalar", lambda sc, dd=recb, a=recf:
                            sc.activation(dd, a[0:64, :], AF.Exp, scale=-1.0),
                            waits=[rec_war2.get((m, rsl))], inc="cAct")
                bset(PRW[h % 2], eL)
                eM = sch.op("vector", lambda v, m_=m, p_=p, a=recb:
                            v.tensor_tensor(mrg[m_ * 64:(m_ + 1) * 64, p_ * TT:(p_ + 1) * TT],
                                            attb[m_ * 64:(m_ + 1) * 64, (p_ % 2) * TT:(p_ % 2) * TT + TT],
                                            a, ALU.mult),
                            waits=[eEv, eR, prev_wo_pe], inc="cDve")
                prev_wo_pe = None
                rec_war[rsl] = eR
                rec_war2[(m, rsl)] = eM
                attb_war[slot] = eM
                mlast = eM
            for cO in range(NC8):
                par = cO % 2
                wts = [mlast, evX] + ro + bfree(PY[par])
                for cI in range(NC8):
                    eO = sch.op("tensor", lambda t, cI_=cI, cO_=cO, p_=PY[par]:
                                t.matmul(p_.ap(),
                                         wqs[:, cI_ * D + cO_ * 128: cI_ * D + cO_ * 128 + 128],
                                         mrg[:, cI_ * TT:(cI_ + 1) * TT],
                                         start=(cI_ == 0), stop=(cI_ == NC8 - 1)),
                                waits=wts if cI == 0 else [],
                                inc=("cPe" if cI == NC8 - 1 else None))
                eOb = sch.op("scalar", lambda sc, p_=PY[par], cO_=cO, dd=obuf[par]:
                             sc.activation(dd, p_.ap(), AF.Identity, bias=boc.ap()[:, cO_:cO_ + 1]),
                             waits=[eO, obuf_war.get(par)], inc="cAct")
                bset(PY[par], eOb)
                eU2 = sch.op("vector", lambda v, dd=u2b[par], a=obuf[par], c_=cO:
                             v.tensor_tensor(dd, u1b[:, c_ * TT:(c_ + 1) * TT], a, ALU.add),
                             waits=[eOb, u2_war.get(par)], inc="cDve")
                obuf_war[par] = eU2
                eD = qdma("scalar", lambda s, c_=cO, T_=T, a=u2b[par]:
                          s.dma_start(u2T.ap()[c_ * 128:(c_ + 1) * 128, T_ * TT:(T_ + 1) * TT], a),
                          waits=[eU2], inc=("dOut" if par == 0 else "dO2"))
                u2_war[par] = eD
            prev_wo_pe = sch.ev("cPe")
            dv = sch.ev("cDve")
            for c in range(NC8):
                ln_state["xfree"][("B", c)] = [dv]

    if DEBUG_DUMPS:
        dbg_ln = nc.dram_tensor("dbg_ln", [128, NC8 * HALF], BF16, kind="ExternalOutput")
        dbg_y = nc.dram_tensor("dbg_y", [128, NC8 * HALF], BF16, kind="ExternalOutput")
        dbg_rw = nc.dram_tensor("dbg_rw", [128, 4 * TT], BF16, kind="ExternalOutput")
        dbg_g = nc.dram_tensor("dbg_g", [128, 160], F32, kind="ExternalOutput")
        dbg_gb = nc.dram_tensor("dbg_gb", [128, 32], BF16, kind="ExternalOutput")
    if "A" in PHASES:
        emit_ffn_phase(xT, u1T, "dX", True)
        if DEBUG_DUMPS:
            wts = [sch.ev("cPe"), sch.ev("cAct"), sch.ev("cDve")]
            sch.dma(lambda s: s.dma_start(dbg_ln.ap(), lnA), waits=wts, inc="dOut")
            sch.dma(lambda s: s.dma_start(dbg_y.ap(), yac), inc="dOut")
            sch.dma(lambda s: s.dma_start(dbg_rw.ap(), rwfm), inc="dOut")
            sch.dma(lambda s: s.dma_start(dbg_g.ap(), gpool.ap()), inc="dOut")
            sch.dma(lambda s: s.dma_start(dbg_gb.ap(), gpoolb.ap()), inc="dOut")
    sch.barrier()
    sch.new_segment()
    qf_ev = kvx_ev = None
    if "B1" in PHASES:
        qf_ev, kvx_ev = emit_B1()
    if "B2" in PHASES:
        emit_B2(qf_ev, kvx_ev)
    sch.barrier()
    sch.new_segment()
    if "C" in PHASES:
        emit_ffn_phase(u2T, outT, "dOut", False)
    sch.op("sync", lambda s: s.nop(), waits=[sch.ev("dOut"), sch.ev("dO2")])

    hw = {e: {} for e in ENGINES}
    with ExitStack() as st:
        semh = {n: st.enter_context(nc.semaphore(n)) for n in SEM_NAMES}

        def mk(engname, seg):
            def run(eng):
                for waits, emit, inc, inc_by in seg[engname]:
                    for (s_, v_) in waits:
                        if v_ > hw[engname].get(s_, 0):
                            eng.wait_ge(semh[s_], v_)
                            hw[engname][s_] = v_
                    inst = emit(eng)
                    if inc is not None:
                        inst.then_inc(semh[inc], inc_by)
            return run

        for i, seg in enumerate(sch.segments):
            with nc.Block(f"seg{i}") as blk:
                blk.sync(mk("sync", seg))
                blk.tensor(mk("tensor", seg))
                blk.vector(mk("vector", seg))
                blk.scalar(mk("scalar", seg))
                blk.gpsimd(mk("gpsimd", seg))
    return nc


def _prep_inputs(inputs):
    f32 = np.float32
    bf16 = ml_dtypes.bfloat16
    g = lambda k: np.asarray(inputs[k], f32)
    wq, bq, wk, bk = g("L0_attn_wq"), g("L0_attn_bq"), g("L0_attn_wk"), g("L0_attn_bk")
    wv, bv, wo, bo = g("L0_attn_wv"), g("L0_attn_bv"), g("L0_attn_wo"), g("L0_attn_bo")
    gw, gb = g("L0_ffn_gate_w"), g("L0_ffn_gate_b")
    w1, b1, w3, b3 = g("L0_ffn_w1"), g("L0_ffn_b1"), g("L0_ffn_w3"), g("L0_ffn_b3")
    w2, b2 = g("L0_ffn_w2"), g("L0_ffn_b2")
    sw1, sb1, sw3, sb3 = g("L0_ffn_sw1"), g("L0_ffn_sb1"), g("L0_ffn_sw3"), g("L0_ffn_sb3")
    sw2, sb2 = g("L0_ffn_sw2"), g("L0_ffn_sb2")
    onehot = np.zeros((NE, D), bf16)
    for e in range(NE):
        onehot[e, e * 128:(e + 1) * 128] = 1
    common = {
        "wqT": np.ascontiguousarray(wq.T), "wkT": np.ascontiguousarray(wk.T),
        "wvT": np.ascontiguousarray(wv.T), "woT": np.ascontiguousarray(wo.T),
        "gwT": np.ascontiguousarray(gw.T),
        "w1d": np.ascontiguousarray(w1.reshape(NE * D, HD)),
        "w3d": np.ascontiguousarray(w3.reshape(NE * D, HD)),
        "w2d": np.ascontiguousarray(w2.reshape(NE * HD, D)),
        "sw1d": sw1, "sw3d": sw3, "sw2d": sw2,
        "b1c_d": np.ascontiguousarray(b1.reshape(NE, NJ, 128).transpose(2, 0, 1).reshape(128, NE * NJ)),
        "b3c_d": np.ascontiguousarray(b3.reshape(NE, NJ, 128).transpose(2, 0, 1).reshape(128, NE * NJ)),
        "sb1c_d": np.ascontiguousarray(sb1.reshape(NJ, 128).T),
        "sb3c_d": np.ascontiguousarray(sb3.reshape(NJ, 128).T),
        "b2": b2, "sb2": sb2.reshape(1, D),
        "bqc_d": np.ascontiguousarray(bq.reshape(NC8, 128).T),
        "boc_d": np.ascontiguousarray(bo.reshape(NC8, 128).T),
        "bkr": bk.reshape(1, D), "bvr": bv.reshape(1, D),
        "gbr_d": gb.reshape(1, NE),
        "onehot_d": onehot, "id128_d": np.eye(128, dtype=bf16),
    }
    x = np.asarray(inputs["L0_attn_x"], f32)
    in_maps = []
    for c in range(NCORES):
        m = dict(common)
        m["xT"] = np.ascontiguousarray(x[c].T)
        in_maps.append(m)
    return in_maps


def kernel(**inputs):
    in_maps = _prep_inputs(inputs)
    nc = build_program()
    res = run_bass_kernel_spmd(nc, in_maps, core_ids=list(range(NCORES)))
    out = np.empty((NCORES, S, D), np.float32)
    for c in range(NCORES):
        out[c] = res.results[c]["outT"].T
    return out
